# version 10 (final)
# baseline (speedup 1.0000x reference)
import numpy as np
import ml_dtypes
from contextlib import ExitStack

import concourse.bass as bass
import concourse.mybir as mybir
from concourse.bass_utils import run_bass_kernel_spmd

F32 = mybir.dt.float32
BF16 = mybir.dt.bfloat16
I32 = mybir.dt.int32
AF = mybir.ActivationFunctionType
ALU = mybir.AluOpType
AX = mybir.AxisListType
BF = ml_dtypes.bfloat16

D = 4096
HD0 = 1024



class Sem:
    def __init__(self, ctx, nc, name):
        self.h = ctx.enter_context(nc.semaphore(name))
        self.name = name
        self.n = 0


class Prog:

    ENGINES = ("sync", "vector", "scalar", "tensor", "gpsimd")

    def __init__(self):
        self.ops = {e: [] for e in self.ENGINES}
        self.lastwait = {e: {} for e in self.ENGINES}
        self.esem = {}

    def serialize(self, vector=None, scalar=None):
        if vector is not None:
            self.esem["vector"] = vector
        if scalar is not None:
            self.esem["scalar"] = scalar

    def op(self, eng, fn, waits=(), inc=None, inck=1):
        es = self.esem.get(eng)
        wl = []
        if es is not None and es.n > 0:
            wl.append((es, es.n))
            self.lastwait[eng][es.name] = es.n
        for s, v in waits:
            if v <= 0:
                continue
            lw = self.lastwait[eng].get(s.name, -1)
            if v > lw:
                wl.append((s, v))
                self.lastwait[eng][s.name] = v
        if es is not None:
            assert inc is None or inc is es
            inc, inck = es, 1
        self.ops[eng].append((wl, fn, inc, inck))
        if inc is not None:
            inc.n += inck
            return inc.n
        return None

    def barrier(self, SP, extra=()):
        waits = [(SP("s_v"), SP("s_v").n), (SP("s_s"), SP("s_s").n),
                 (SP("s_t"), SP("s_t").n)] + list(extra)
        for ename in self.ENGINES:
            self.op(ename, lambda e: e.nop(), waits=list(waits))

    def emit(self, block):
        for ename in self.ENGINES:
            oplist = self.ops[ename]

            def body(eng, oplist=oplist):
                for wl, fn, inc, inck in oplist:
                    for s, v in wl:
                        eng.wait_ge(s.h, v)
                    ins = fn(eng)
                    if inc is not None:
                        ins.then_inc(inc.h, inck)

            getattr(block, ename)(body)


class SemPool:
    def __init__(self, ctx, nc):
        self.ctx, self.nc, self.d = ctx, nc, {}

    def __call__(self, name):
        if name not in self.d:
            self.d[name] = Sem(self.ctx, self.nc, name)
        return self.d[name]


def new_prog(T=None, NT=None):
    nc = bass.Bass("TRN2", target_bir_lowering=False, debug=False)
    return nc, Prog()



def _pfx(nc, pre):
    def sb(name, shape, dt):
        return nc.sbuf_tensor(pre + name, shape, dt)

    def ps(name, shape, dt):
        return nc.psum_tensor(pre + name, shape, dt)

    return sb, ps



def emit_attn0(nc, p, SP, T, xb, bvq8, bvk8, bvv1, ident, identf, triu, attn0,
               vbuf):
    NK = T // 128
    NC = T // 256
    sb, ps = _pfx(nc, "A_")
    with ExitStack() as ctx:
        E = ctx.enter_context
        xnT = E(sb("xnT", [128, 8 * T], BF16))
        id_s = E(sb("id_s", [128, 128], BF16))
        idf_s = E(sb("idf_s", [128, 128], F32))
        triu_s = E(sb("triu_s", [128, 128], BF16))
        bq_s = E(sb("bq_s", [8, 128], F32))
        bk_s = E(sb("bk_s", [8, 128], F32))
        bv_s = E(sb("bv_s", [1, HD0], F32))
        sq_s = E(sb("sq_s", [8, 128], F32))
        sk_s = E(sb("sk_s", [8, 128], F32))
        g8_s = E(sb("g8_s", [8, 128], BF16))
        gcol = E(sb("gcol", [128, 8], F32))
        apart = E(sb("apart", [8, 2], F32))
        ones8 = E(sb("ones8", [8, 1], F32))
        ones1 = E(sb("ones1", [1, 128], F32))
        asum = E(sb("asum", [1, 2], F32))
        acomb = E(sb("acomb", [1, 1], F32))
        sc128 = E(sb("sc128", [128, 1], F32))
        eps_s = E(sb("eps_s", [128, 1], F32))
        vbind = E(sb("vbind", [1, HD0], F32))
        vb_s = E(sb("vb_s", [128, HD0], F32))
        scr2 = E(sb("scr2", [1, 2], F32))
        x_s = [E(sb(f"x_s{i}", [128, D], F32)) for i in range(2)]
        stat = [E(sb(f"stat{i}", [128, 48], F32)) for i in range(2)]
        aggr = [E(sb(f"aggr{i}", [128, 4], F32)) for i in range(2)]
        xn_s = [E(sb(f"xn_s{i}", [128, HD0], F32)) for i in range(2)]
        v_sb = [E(sb(f"v_sb{i}", [128, HD0], BF16)) for i in range(2)]
        qq_s = [E(sb("qq_s0", [128, 8 * 256], BF16))] * 2
        vk_s = [E(sb(f"vk_s{i}", [128, HD0], BF16)) for i in range(2)]
        at_s = [E(sb(f"at_s{i}", [128, 256], BF16)) for i in range(2)]
        oev = [E(sb("oev0", [128, 2 * HD0], F32))] * 2

        s_ld = SP("A_ld")
        s_x = [SP(f"A_x{i}") for i in range(2)]
        s_vs = [SP(f"A_vs{i}") for i in range(2)]
        s_vl = [SP(f"A_vl{i}") for i in range(2)]
        s_a0 = SP("A_a0")
        s_a1 = SP("A_a1")
        s_v = SP("s_v")
        s_s = SP("s_s")
        s_t = SP("s_t")
        p.serialize(vector=s_v, scalar=s_s)

        with ExitStack() as psA:
            p_a = psA.enter_context(ps("p_a", [1, 2], F32))
            p_b = psA.enter_context(ps("p_b", [128, 1], F32))
            p_vb = psA.enter_context(ps("p_vb", [128, HD0], F32))
            p_g = psA.enter_context(ps("p_g", [128, 8], BF16))
            p_tr = psA.enter_context(ps("p_tr", [128, 8 * 128], F32))

            p.op("sync", lambda e: e.dma_start(out=id_s[:], in_=ident), inc=s_ld, inck=16)
            p.op("sync", lambda e: e.dma_start(out=idf_s[:], in_=identf), inc=s_ld, inck=16)
            p.op("sync", lambda e: e.dma_start(out=triu_s[:], in_=triu), inc=s_ld, inck=16)
            p.op("sync", lambda e: e.dma_start(out=bq_s[:], in_=bvq8), inc=s_ld, inck=16)
            p.op("sync", lambda e: e.dma_start(out=bk_s[:], in_=bvk8), inc=s_ld, inck=16)
            ld0 = p.op("sync", lambda e: e.dma_start(out=bv_s[:], in_=bvv1),
                       inc=s_ld, inck=16)

            p.op("vector", lambda e: e.memset(ones8[:], 1.0), waits=[(s_ld, ld0)])
            p.op("vector", lambda e: e.memset(ones1[:], 1.0))
            v_eps = p.op("vector", lambda e: e.memset(eps_s[:], 1e-5), inc=s_v)
            p.op("vector", lambda e: e.tensor_reduce(
                apart[:, 0:1], bq_s[:], axis=AX.X, op=ALU.add, apply_absolute_value=True))
            va = p.op("vector", lambda e: e.tensor_reduce(
                apart[:, 1:2], bk_s[:], axis=AX.X, op=ALU.add, apply_absolute_value=True),
                inc=s_v)
            t_a = p.op("tensor", lambda e: e.matmul(p_a[:], ones8[:], apart[:],
                                                    start=True, stop=True),
                       waits=[(s_v, va)], inc=s_t)
            c_a = p.op("scalar", lambda e: e.copy(asum[:], p_a[:]),
                       waits=[(s_t, t_a)], inc=s_s)
            p.op("vector", lambda e: e.tensor_tensor(acomb[:], asum[:, 0:1],
                                                     asum[:, 1:2], op=ALU.mult),
                 waits=[(s_s, c_a)])
            v_ac = p.op("vector", lambda e: e.tensor_scalar(
                acomb[:], acomb[:], 0.125 / (HD0 * HD0), None, op0=ALU.mult), inc=s_v)
            t_b = p.op("tensor", lambda e: e.matmul(p_b[:], ones1[:], acomb[:],
                                                    start=True, stop=True),
                       waits=[(s_v, v_ac)], inc=s_t)
            c_b = p.op("scalar", lambda e: e.copy(sc128[:], p_b[:]),
                       waits=[(s_t, t_b)], inc=s_s)

            p.op("vector", lambda e: e.tensor_scalar(sq_s[:], bq_s[:], 0.0, None,
                                                     op0=ALU.is_lt))
            p.op("vector", lambda e: e.tensor_scalar(sq_s[:], sq_s[:], -2.0, 1.0,
                                                     op0=ALU.mult, op1=ALU.add))
            p.op("vector", lambda e: e.tensor_scalar(sk_s[:], bk_s[:], 0.0, None,
                                                     op0=ALU.is_lt))
            p.op("vector", lambda e: e.tensor_scalar(sk_s[:], sk_s[:], -2.0, 1.0,
                                                     op0=ALU.mult, op1=ALU.add))
            v_g8 = p.op("vector", lambda e: e.tensor_tensor(g8_s[:], sq_s[:], sk_s[:],
                                                            op=ALU.mult), inc=s_v)
            t_g = p.op("tensor", lambda e: e.transpose(p_g[:], g8_s[:], id_s[0:8, 0:8]),
                       waits=[(s_v, v_g8)], inc=s_t)
            c_g = p.op("scalar", lambda e: e.copy(gcol[:], p_g[:]),
                       waits=[(s_t, t_g)], inc=s_s)

            v_r = p.op("vector", lambda e: e.tensor_reduce(
                scr2[:, 0:1], bv_s[:], axis=AX.X, op=ALU.add, apply_absolute_value=True),
                inc=s_v)
            p.op("vector", lambda e: e.tensor_scalar(scr2[:, 0:1], scr2[:, 0:1],
                                                     1.0 / HD0, None, op0=ALU.mult),
                 waits=[(s_v, v_r)])
            p.op("vector", lambda e: e.tensor_scalar(vbind[:], bv_s[:], 0.0, None,
                                                     op0=ALU.is_lt))
            p.op("vector", lambda e: e.tensor_scalar(vbind[:], vbind[:], -2.0, 1.0,
                                                     op0=ALU.mult, op1=ALU.add))
            v_vb = p.op("vector", lambda e: e.tensor_scalar(
                vbind[:], vbind[:], scr2[:, 0:1], None, op0=ALU.mult), inc=s_v)
            p.op("tensor", lambda e: e.matmul(p_vb[:, 0:512], ones1[:], vbind[:, 0:512],
                                              start=True, stop=True),
                 waits=[(s_v, v_vb)])
            t_vb = p.op("tensor", lambda e: e.matmul(p_vb[:, 512:1024], ones1[:],
                                                     vbind[:, 512:1024],
                                                     start=True, stop=True), inc=s_t)
            c_vb = p.op("scalar", lambda e: e.copy(vb_s[:], p_vb[:]),
                        waits=[(s_t, t_vb)], inc=s_s)

            m_xld = {}
            m_nrm = {}
            m_tr = {}
            m_xc = {}
            hs = 0
            for k in range(NK):
                b = k % 2
                w = [] if k < 2 else [(s_v, m_nrm[k - 2])]
                m_xld[k] = p.op(
                    "sync",
                    lambda e, k=k, b=b: e.dma_start(out=x_s[b][:],
                                                    in_=xb[k * 128:(k + 1) * 128, :]),
                    waits=w, inc=s_x[b], inck=16)

                for i in range(8):
                    ins = p.op("vector",
                               lambda e, b=b, i=i: e.bn_stats(
                                   stat[b][:, 6 * i:6 * i + 6],
                                   x_s[b][:, 512 * i:512 * (i + 1)]),
                               waits=[(s_x[b], m_xld[k])],
                               inc=s_v if i == 7 else None)
                    if i == 7:
                        v_bns = ins
                v_bna = p.op("vector", lambda e, b=b: e.bn_aggr(aggr[b][:, 0:2],
                                                                stat[b][:]),
                             waits=[(s_v, v_bns)], inc=s_v)
                s_sd = p.op("scalar", lambda e, b=b: e.activation(
                    aggr[b][:, 2:3], aggr[b][:, 1:2], AF.Sqrt, bias=eps_s[:, 0:1],
                    scale=1.0),
                    waits=[(s_v, v_bna), (s_v, v_eps)], inc=s_s)
                v_rc = p.op("vector", lambda e, b=b: e.reciprocal(
                    aggr[b][:, 3:4], aggr[b][:, 2:3]), waits=[(s_s, s_sd)], inc=s_v)
                w = [(s_v, v_rc)]
                if k >= 2:
                    w.append((s_t, m_tr[k - 2]))
                    w.append((s_vs[b], 16 * ((k - 2) // 2 + 1)))
                m_nrm[k] = p.op("vector", lambda e, b=b: e.tensor_scalar(
                    xn_s[b][:], x_s[b][:, hs:hs + HD0], aggr[b][:, 0:1],
                    aggr[b][:, 3:4], op0=ALU.subtract, op1=ALU.mult),
                    waits=w, inc=s_v)
                v_v = p.op("vector", lambda e, b=b: e.tensor_tensor(
                    v_sb[b][:], xn_s[b][:], vb_s[:], op=ALU.mult),
                    waits=[(s_s, c_vb)], inc=s_v)
                p.op("gpsimd", lambda e, b=b, k=k: e.dma_start(
                    out=vbuf[k * 128:(k + 1) * 128, :], in_=v_sb[b][:]),
                    waits=[(s_v, v_v)], inc=s_vs[b], inck=16)
                w = [(s_v, m_nrm[k])]
                if k >= 1:
                    w.append((s_s, m_xc[k - 1]))
                for dt in range(8):
                    ins = p.op("tensor", lambda e, b=b, dt=dt: e.transpose(
                        p_tr[:, dt * 128:(dt + 1) * 128],
                        xn_s[b][:, dt * 128:(dt + 1) * 128], idf_s[:]),
                        waits=w if dt == 0 else (), inc=s_t if dt == 7 else None)
                m_tr[k] = ins
                dst = xnT[:].rearrange("p (dt t) -> p dt t", dt=8)[:, :, k * 128:(k + 1) * 128]
                m_xc[k] = p.op("scalar", lambda e, dst=dst: e.copy(
                    dst, p_tr[:].rearrange("p (dt t) -> p dt t", dt=8)),
                    waits=[(s_t, m_tr[k])], inc=s_s)

        with ExitStack() as psB:
            p_sc = [psB.enter_context(ps(f"p_sc{i}", [128, 256], F32))
                    for i in range(2)]
            p_out = [psB.enter_context(ps(f"p_out{i}", [128, HD0], F32))
                     for i in range(2)]

            m_qq = {}
            m_scmm = {}
            m_sig = {}
            m_diag = {}
            m_av = {}
            m_evs = {}
            m_evv = {}
            m_vld = {}
            xnT3 = xnT[:].rearrange("p (dt t) -> p dt t", dt=8)
            for qc in range(NC):
                w = [(s_s, m_xc[NK - 1]), (s_s, c_g)]
                if qc >= 1:
                    w.append((s_t, m_scmm[qc - 1]))
                for dt in range(8):
                    ins = p.op("vector", lambda e, dt=dt, qc=qc: e.tensor_scalar(
                        qq_s[qc % 2][:, dt * 256:(dt + 1) * 256],
                        xnT3[:, dt, qc * 256:(qc + 1) * 256],
                        gcol[:, dt:dt + 1], None, op0=ALU.mult),
                        waits=w if dt == 0 else (), inc=s_v if dt == 7 else None)
                m_qq[qc] = ins

                nk_q = 2 * qc + 2
                for k in range(nk_q):
                    vis = qc * qc + qc + k
                    ab = vis % 2
                    w = [(s_vs[0], 16 * ((NK + 1) // 2)), (s_vs[1], 16 * (NK // 2))]
                    if vis >= 2:
                        w.append((s_t, m_av[vis - 2]))
                    m_vld[vis] = p.op("sync", lambda e, k=k, ab=ab: e.dma_start(
                        out=vk_s[ab][:], in_=vbuf[k * 128:(k + 1) * 128, :]),
                        waits=w, inc=s_vl[ab], inck=16)
                    w = [(s_v, m_qq[qc])]
                    if vis >= 2:
                        w.append((s_s, m_sig[vis - 2]))
                    for dt in range(8):
                        ins = p.op("tensor", lambda e, dt=dt, k=k, qc=qc, ab=ab: e.matmul(
                            p_sc[ab][:],
                            xnT3[:, dt, k * 128:(k + 1) * 128],
                            qq_s[qc % 2][:, dt * 256:(dt + 1) * 256],
                            start=(dt == 0), stop=(dt == 7)),
                            waits=w if dt == 0 else (),
                            inc=s_t if dt == 7 else None)
                    m_scmm_vis = ins
                    if k == nk_q - 1:
                        m_scmm[qc] = ins
                    w = [(s_t, m_scmm_vis), (s_s, c_b)]
                    if vis >= 2:
                        w.append((s_t, m_av[vis - 2]))
                    m_sig[vis] = p.op("scalar", lambda e, ab=ab: e.activation(
                        at_s[ab][:], p_sc[ab][:], AF.Sigmoid, scale=sc128[:, 0:1]),
                        waits=w, inc=s_s)
                    diag = None
                    if k == 2 * qc:
                        diag = p.op("vector", lambda e, ab=ab: e.tensor_tensor(
                            at_s[ab][:, 0:128], at_s[ab][:, 0:128], triu_s[:],
                            op=ALU.mult), waits=[(s_s, m_sig[vis])], inc=s_v)
                    elif k == 2 * qc + 1:
                        p.op("vector", lambda e, ab=ab: e.memset(at_s[ab][:, 0:128], 0.0),
                             waits=[(s_s, m_sig[vis])])
                        diag = p.op("vector", lambda e, ab=ab: e.tensor_tensor(
                            at_s[ab][:, 128:256], at_s[ab][:, 128:256], triu_s[:],
                            op=ALU.mult), inc=s_v)
                    m_diag[vis] = diag
                    w = [(s_vl[vis % 2], m_vld[vis])]
                    if diag is not None:
                        w.append((s_v, diag))
                    else:
                        w.append((s_s, m_sig[vis]))
                    if k == 0 and qc >= 1:
                        w.append((s_s, m_evs[qc - 1]))
                        w.append((s_v, m_evv[qc - 1]))
                    for th in range(2):
                        for dh in range(2):
                            last = th == 1 and dh == 1
                            ins = p.op("tensor", lambda e, th=th, dh=dh, ab=ab, k=k, nk_q=nk_q: e.matmul(
                                p_out[th][:, dh * 512:(dh + 1) * 512],
                                at_s[ab][:, th * 128:(th + 1) * 128],
                                vk_s[ab][:, dh * 512:(dh + 1) * 512],
                                start=(k == 0), stop=(k == nk_q - 1)),
                                waits=w if (th == 0 and dh == 0) else (),
                                inc=s_t if last else None)
                    m_av[vis] = ins

                w = [(s_t, m_av[qc * qc + qc + nk_q - 1])]
                if qc >= 1:
                    w.append((s_a0, 16 * qc))
                    w.append((s_a1, 16 * qc))
                m_evs[qc] = p.op("scalar", lambda e, qc=qc: e.copy(
                    oev[qc % 2][:, 0:HD0], p_out[0][:]), waits=w, inc=s_s)
                m_evv[qc] = p.op("vector", lambda e, qc=qc: e.tensor_copy(
                    oev[qc % 2][:, HD0:2 * HD0], p_out[1][:]), waits=w, inc=s_v)
                p.op("gpsimd", lambda e, qc=qc: e.dma_start(
                    out=attn0[qc * 256:qc * 256 + 128, :], in_=oev[qc % 2][:, 0:HD0]),
                    waits=[(s_s, m_evs[qc])], inc=s_a0, inck=16)
                p.op("gpsimd", lambda e, qc=qc: e.dma_start(
                    out=attn0[qc * 256 + 128:qc * 256 + 256, :],
                    in_=oev[qc % 2][:, HD0:2 * HD0]),
                    waits=[(s_v, m_evv[qc])], inc=s_a1, inck=16)

            p.op("gpsimd", lambda e: e.nop(),
                 waits=[(s_a0, 16 * NC), (s_a1, 16 * NC)])
    return NC


def build_attn0(T):
    nc = bass.Bass("TRN2", target_bir_lowering=False, debug=False)
    xb = nc.dram_tensor("xb", [T, D], F32, kind="ExternalInput").ap()
    bvq8 = nc.dram_tensor("bvq8", [8, 128], F32, kind="ExternalInput").ap()
    bvk8 = nc.dram_tensor("bvk8", [8, 128], F32, kind="ExternalInput").ap()
    bvv1 = nc.dram_tensor("bvv1", [1, HD0], F32, kind="ExternalInput").ap()
    ident = nc.dram_tensor("ident", [128, 128], BF16, kind="ExternalInput").ap()
    identf = nc.dram_tensor("identf", [128, 128], F32, kind="ExternalInput").ap()
    triu = nc.dram_tensor("triu", [128, 128], BF16, kind="ExternalInput").ap()
    attn0 = nc.dram_tensor("attn0", [T, HD0], F32, kind="ExternalOutput").ap()
    vbuf = nc.dram_tensor("vbuf", [T, HD0], BF16, kind="Internal").ap()
    p = Prog()
    with ExitStack() as ctx:
        SP = SemPool(ctx, nc)
        emit_attn0(nc, p, SP, T, xb, bvq8, bvk8, bvv1, ident, identf, triu,
                   attn0, vbuf)
        block = ctx.enter_context(nc.Block())
        p.emit(block)
    return nc



_IDENT = np.eye(128, dtype=BF)
_IDENTF = np.eye(128, dtype=np.float32)
_TRIU = np.triu(np.ones((128, 128), dtype=BF))


def run_attn0(x, bv_q, bv_k, bv_v, T=None):
    B, Tx, _ = x.shape
    T = T or Tx
    nc = build_attn0(T)
    in_maps = []
    for r in range(8):
        b, h = r // 4, r % 4
        xr = np.ascontiguousarray(np.roll(x[b], -HD0 * h, axis=1))
        in_maps.append(dict(
            xb=xr,
            bvq8=np.ascontiguousarray(bv_q[h].reshape(8, 128)),
            bvk8=np.ascontiguousarray(bv_k[h].reshape(8, 128)),
            bvv1=np.ascontiguousarray(bv_v[h].reshape(1, HD0)),
            ident=_IDENT, identf=_IDENTF, triu=_TRIU,
        ))
    res = run_bass_kernel_spmd(nc, in_maps, core_ids=list(range(8)))
    out = np.empty((B, T, D), np.float32)
    for r in range(8):
        b, h = r // 4, r % 4
        out[b, :, HD0 * h:HD0 * (h + 1)] = res.results[r]["attn0"]
    return out




def emit_ffn0(nc, p, SP, NT, xo, a0, A_w, B_w, Wq_w, keys, identf, x1, xn1,
              a0g=None, a0idx=None, pre_waits=()):
    Ntok = NT * 128
    sb, ps = _pfx(nc, "B_")
    with ExitStack() as ctx:
        E = ctx.enter_context
        idf_s = E(sb("idf_s", [128, 128], F32))
        id_bf = E(sb("id_bf", [128, 128], BF16))
        eps_s = E(sb("eps_s", [128, 1], F32))
        A_T = E(sb("A_T", [128, 32 * 512], BF16))
        B_T = E(sb("B_T", [128, 4 * D], BF16))
        WqT = E(sb("WqT", [128, 32 * 16], BF16))
        keysT = E(sb("keysT", [16, 64], BF16))
        wq_s = E(sb("wq_s", [16, D], F32))
        wq_bf = E(sb("wq_bf", [16, D], BF16))
        key_s = E(sb("key_s", [64, 16], F32))
        key_bf = E(sb("key_bf", [64, 16], BF16))
        x_s = [E(sb(f"x_s{i}", [128, D], F32)) for i in range(2)]
        a_s = [E(sb("a_s0", [128, D], F32))] * 2
        stat = E(sb("stat", [128, 48], F32))
        aggr = E(sb("aggr", [128, 4], F32))
        xn_f = E(sb("xn_f", [128, D], F32))
        xnT_t = E(sb("xnT_t", [128, D], BF16))
        qT_bf = E(sb("qT_bf", [16, 128], BF16))
        sco_s = E(sb("sco_s", [128, 64], F32))
        zap_s = E(sb("zap_s", [128, 64], F32))
        m16 = E(sb("m16", [128, 16], F32))
        nm1 = E(sb("nm1", [128, 1], F32))
        gmask = E(sb("gmask", [128, 64], F32))
        gexp = E(sb("gexp", [128, 64], F32))
        gden = E(sb("gden", [128, 2], F32))
        gate = E(sb("gate", [128, 64], F32))
        zg_bf = E(sb("zg_bf", [128, 512], BF16))
        zgT = E(sb("zgT", [128, 512], BF16))
        x1_s = [E(sb(f"x1_s{i}", [128, D], F32)) for i in range(2)]
        xn1_s = [E(sb("xn1_s0", [128, D], BF16))] * 2

        s_ld = SP("B_ld")
        s_wld = SP("B_wld")
        s_x = [SP(f"B_x{i}") for i in range(2)]
        s_a = SP("B_a")
        s_st1 = [SP(f"B_st1{i}") for i in range(2)]
        s_st2 = SP("B_st2")
        s_v = SP("s_v")
        s_s = SP("s_s")
        s_t = SP("s_t")
        p.serialize(vector=s_v, scalar=s_s)
        if a0g is not None:
            idx_s = E(sb("a0idx_s", [128, 4 * NT], I32))

        p_tr = E(ps("p_tr", [128, 8 * 128], F32))
        p_trb = E(ps("p_trb", [128, 8 * 128], BF16))
        p_q = E(ps("p_q", [16, 128], F32))
        p_sco = E(ps("p_sco", [128, 64], F32))
        p_z = E(ps("p_z", [128, 512], F32))
        p_f = E(ps("p_f", [128, 512], F32))

        p.op("sync", lambda e: e.dma_start(out=idf_s[:], in_=identf),
             waits=list(pre_waits), inc=s_ld, inck=16)
        p.op("sync", lambda e: e.dma_start(out=wq_s[:], in_=Wq_w), inc=s_ld, inck=16)
        p.op("sync", lambda e: e.dma_start(out=key_s[:], in_=keys), inc=s_ld, inck=16)
        if a0g is not None:
            p.op("sync", lambda e: e.dma_start(out=idx_s[:], in_=a0idx),
                 inc=s_ld, inck=16)
        ld_all = s_ld.n
        v_idb = p.op("vector", lambda e: e.tensor_copy(id_bf[:], idf_s[:]),
                     waits=[(s_ld, ld_all)], inc=s_v)
        p.op("vector", lambda e: e.memset(eps_s[:], 1e-5))
        v_wq = p.op("vector", lambda e: e.tensor_copy(wq_bf[:], wq_s[:]), inc=s_v)
        v_k = p.op("vector", lambda e: e.tensor_copy(key_bf[:], key_s[:]), inc=s_v)
        for dt in range(16):
            ins = p.op("tensor", lambda e, dt=dt: e.transpose(
                p_trb[:, dt * 16:dt * 16 + 16],
                wq_bf[:, dt * 128:(dt + 1) * 128], id_bf[0:16, 0:16]),
                waits=[(s_v, v_wq), (s_v, v_idb)] if dt == 0 else (),
                inc=s_t if dt == 15 else None)
        t_wq1 = ins
        c_wq1 = p.op("scalar", lambda e: e.copy(WqT[:, 0:256], p_trb[:, 0:256]),
                     waits=[(s_t, t_wq1)], inc=s_s)
        for dt in range(16, 32):
            ins = p.op("tensor", lambda e, dt=dt: e.transpose(
                p_trb[:, (dt - 16) * 16:(dt - 16) * 16 + 16],
                wq_bf[:, dt * 128:(dt + 1) * 128], id_bf[0:16, 0:16]),
                waits=[(s_s, c_wq1)] if dt == 16 else (),
                inc=s_t if dt == 31 else None)
        t_wq2 = ins
        c_wq2 = p.op("scalar", lambda e: e.copy(WqT[:, 256:512], p_trb[:, 0:256]),
                     waits=[(s_t, t_wq2)], inc=s_s)
        t_k = p.op("tensor", lambda e: e.transpose(p_trb[0:16, 0:64], key_bf[:],
                                                   id_bf[0:64, 0:64]),
                   waits=[(s_v, v_k), (s_s, c_wq2)], inc=s_t)
        c_k = p.op("scalar", lambda e: e.copy(keysT[:], p_trb[0:16, 0:64]),
                   waits=[(s_t, t_k)], inc=s_s)

        wld = x_s[0]
        wld_bf = xnT_t
        prev_cp = c_k
        for i in range(4):
            ldw = p.op("sync", lambda e, i=i: e.dma_start(
                out=wld[:], in_=A_w[i * 128:(i + 1) * 128, :]),
                waits=[(s_s, prev_cp)], inc=s_wld, inck=16)
            vcp = p.op("vector", lambda e: e.tensor_copy(wld_bf[:], wld[:]),
                       waits=[(s_wld, ldw)], inc=s_v)
            for dt in range(32):
                ins = p.op("tensor", lambda e, dt=dt: e.transpose(
                    p_trb[:, (dt % 8) * 128:(dt % 8 + 1) * 128],
                    wld_bf[:, dt * 128:(dt + 1) * 128], id_bf[:]),
                    waits=([(s_v, vcp)] if dt == 0 else
                           ([(s_s, s_s.n)] if dt % 8 == 0 else ())),
                    inc=s_t if dt % 8 == 7 else None)
                if dt % 8 == 7:
                    grp = dt // 8
                    dst = A_T[:].rearrange("p (dt c) -> p dt c", c=512)[
                        :, grp * 8:(grp + 1) * 8, i * 128:(i + 1) * 128]
                    prev_cp = p.op("scalar", lambda e, dst=dst: e.copy(
                        dst, p_trb[:].rearrange("p (dt c) -> p dt c", c=128)),
                        waits=[(s_t, ins)], inc=s_s)
        for i in range(32):
            ldw = p.op("sync", lambda e, i=i: e.dma_start(
                out=wld[:, 0:512], in_=B_w[i * 128:(i + 1) * 128, :]),
                waits=[(s_s, prev_cp)], inc=s_wld, inck=16)
            vcp = p.op("vector", lambda e: e.tensor_copy(wld_bf[:, 0:512],
                                                         wld[:, 0:512]),
                       waits=[(s_wld, ldw)], inc=s_v)
            for nb in range(4):
                ins = p.op("tensor", lambda e, nb=nb: e.transpose(
                    p_trb[:, nb * 128:(nb + 1) * 128],
                    wld_bf[:, nb * 128:(nb + 1) * 128], id_bf[:]),
                    waits=[(s_v, vcp)] if nb == 0 else (),
                    inc=s_t if nb == 3 else None)
            dst = B_T[:].rearrange("p (nb c) -> p nb c", c=D)[
                :, :, i * 128:(i + 1) * 128]
            prev_cp = p.op("scalar", lambda e, dst=dst: e.copy(
                dst, p_trb[:, 0:512].rearrange("p (nb c) -> p nb c", c=128)),
                waits=[(s_t, ins)], inc=s_s)
        prep_done = prev_cp

        m_nrm = {}
        m_sq = {}
        m_trl = {}
        m_zg = {}
        m_xa = {}
        for t in range(NT):
            b = t % 2
            if t < 2:
                w = [(s_s, prep_done)]
            else:
                w = [(s_v, m_xa[t - 2])]
            ldx = p.op("sync", lambda e, t=t, b=b: e.dma_start(
                out=x_s[b][:], in_=xo[t * 128:(t + 1) * 128, :]), waits=w,
                inc=s_x[b], inck=16)
            w = [(s_ld, ld_all)] if t < 1 else [(s_v, m_xa[t - 1])]
            if a0g is None:
                lda = p.op("sync", lambda e, t=t, b=b: e.dma_start(
                    out=a_s[b][:], in_=a0[t * 128:(t + 1) * 128, :]),
                    waits=w, inc=s_a, inck=16)
            else:
                for h in range(4):
                    lda = p.op("gpsimd", lambda e, t=t, b=b, h=h: e.indirect_dma_start(
                        a_s[b][:, h * 1024:(h + 1) * 1024], None,
                        a0g, bass.IndirectOffsetOnAxis(
                            ap=idx_s[:, t * 4 + h:t * 4 + h + 1], axis=0)),
                        waits=w if h == 0 else (), inc=s_a, inck=16)
            for i in range(8):
                ins = p.op("vector", lambda e, b=b, i=i: e.bn_stats(
                    stat[:, 6 * i:6 * i + 6], x_s[b][:, 512 * i:512 * (i + 1)]),
                    waits=[(s_x[b], ldx)], inc=s_v if i == 7 else None)
            v_bns = ins
            v_bna = p.op("vector", lambda e: e.bn_aggr(aggr[:, 0:2], stat[:]),
                         waits=[(s_v, v_bns)], inc=s_v)
            s_sd = p.op("scalar", lambda e: e.activation(
                aggr[:, 2:3], aggr[:, 1:2], AF.Sqrt, bias=eps_s[:, 0:1], scale=1.0),
                waits=[(s_v, v_bna)], inc=s_s)
            v_rc = p.op("vector", lambda e: e.reciprocal(aggr[:, 3:4], aggr[:, 2:3]),
                        waits=[(s_s, s_sd)], inc=s_v)
            w = [(s_v, v_rc)]
            if t >= 1:
                w.append((s_t, m_trl[t - 1]))
            m_nrm[t] = p.op("vector", lambda e, b=b: e.tensor_scalar(
                xn_f[:], x_s[b][:], aggr[:, 0:1], aggr[:, 3:4],
                op0=ALU.subtract, op1=ALU.mult), waits=w, inc=s_v)
            for rnd in range(4):
                for j in range(8):
                    dt = rnd * 8 + j
                    wts = []
                    if rnd == 0 and j == 0:
                        wts = [(s_v, m_nrm[t]), (s_s, prep_done if t == 0 else m_sq[t - 1])]
                    elif j == 0:
                        wts = [(s_s, s_s.n)]
                    ins = p.op("tensor", lambda e, dt=dt, j=j: e.transpose(
                        p_tr[:, j * 128:(j + 1) * 128],
                        xn_f[:, dt * 128:(dt + 1) * 128], idf_s[:]),
                        waits=wts, inc=s_t if j == 7 else None)
                dst = xnT_t[:].rearrange("p (dt c) -> p dt c", c=128)[
                    :, rnd * 8:(rnd + 1) * 8, :]
                cp = p.op("scalar", lambda e, dst=dst: e.copy(
                    dst, p_tr[:].rearrange("p (dt c) -> p dt c", c=128)),
                    waits=[(s_t, ins)], inc=s_s)
            m_xt = cp
            m_trl[t] = s_t.n
            for dt in range(32):
                ins = p.op("tensor", lambda e, dt=dt: e.matmul(
                    p_q[:], WqT[:, dt * 16:(dt + 1) * 16],
                    xnT_t[:, dt * 128:(dt + 1) * 128],
                    start=(dt == 0), stop=(dt == 31)),
                    waits=[(s_s, m_xt)] if dt == 0 else (),
                    inc=s_t if dt == 31 else None)
            t_q = ins
            c_q = p.op("scalar", lambda e: e.copy(qT_bf[:], p_q[:]),
                       waits=[(s_t, t_q)], inc=s_s)
            t_sco = p.op("tensor", lambda e: e.matmul(
                p_sco[:], qT_bf[:], keysT[:], start=True, stop=True),
                waits=[(s_s, c_q)], inc=s_t)
            c_sco = p.op("scalar", lambda e: e.copy(sco_s[:], p_sco[:]),
                         waits=[(s_t, t_sco)], inc=s_s)
            v_m1 = p.op("vector", lambda e: e.max(out=m16[:, 0:8], in_=sco_s[:]),
                        waits=[(s_s, c_sco)], inc=s_v)
            v_zap = p.op("vector", lambda e: e.match_replace(
                out=zap_s[:], in_to_replace=m16[:, 0:8], in_values=sco_s[:],
                imm_value=-1e30), waits=[(s_v, v_m1)], inc=s_v)
            v_m2 = p.op("vector", lambda e: e.max(out=m16[:, 8:16], in_=zap_s[:]),
                        waits=[(s_v, v_zap)], inc=s_v)
            v_nm = p.op("vector", lambda e: e.tensor_scalar(
                nm1[:], m16[:, 0:1], -1.0, None, op0=ALU.mult),
                waits=[(s_v, v_m2)], inc=s_v)
            v_msk = p.op("vector", lambda e: e.tensor_scalar(
                gmask[:], sco_s[:], m16[:, 15:16], None, op0=ALU.is_ge),
                inc=s_v)
            s_exp = p.op("scalar", lambda e: e.activation(
                gexp[:], sco_s[:], AF.Exp, bias=nm1[:, 0:1], scale=1.0),
                waits=[(s_v, v_nm)], inc=s_s)
            v_gn = p.op("vector", lambda e: e.tensor_tensor(
                gexp[:], gexp[:], gmask[:], op=ALU.mult),
                waits=[(s_s, s_exp), (s_v, v_msk)], inc=s_v)
            v_dn = p.op("vector", lambda e: e.tensor_reduce(
                gden[:, 0:1], gexp[:], axis=AX.X, op=ALU.add),
                waits=[(s_v, v_gn)], inc=s_v)
            v_rc2 = p.op("vector", lambda e: e.reciprocal(gden[:, 1:2], gden[:, 0:1]),
                         waits=[(s_v, v_dn)], inc=s_v)
            v_gt = p.op("vector", lambda e: e.tensor_scalar(
                gate[:], gexp[:], gden[:, 1:2], 2.0, op0=ALU.mult, op1=ALU.mult),
                waits=[(s_v, v_rc2)], inc=s_v)
            for dt in range(32):
                w = []
                if dt == 0 and t >= 1:
                    w = [(s_v, m_zg[t - 1])]
                ins = p.op("tensor", lambda e, dt=dt: e.matmul(
                    p_z[:], xnT_t[:, dt * 128:(dt + 1) * 128],
                    A_T[:, dt * 512:(dt + 1) * 512],
                    start=(dt == 0), stop=(dt == 31)),
                    waits=w, inc=s_t if dt == 31 else None)
            t_z = ins
            gx = gate[:].rearrange("p (o u) -> p o u", u=1).to_broadcast([128, 64, 8])
            v_zg = p.op("vector", lambda e, gx=gx: e.tensor_tensor(
                zg_bf[:].rearrange("p (o u) -> p o u", u=8),
                p_z[:].rearrange("p (o u) -> p o u", u=8), gx, op=ALU.mult),
                waits=[(s_v, v_gt), (s_t, t_z)], inc=s_v)
            m_zg[t] = v_zg
            for nb in range(4):
                ins = p.op("tensor", lambda e, nb=nb: e.transpose(
                    p_trb[:, nb * 128:(nb + 1) * 128],
                    zg_bf[:, nb * 128:(nb + 1) * 128], id_bf[:]),
                    waits=[(s_v, v_zg)] if nb == 0 else (),
                    inc=s_t if nb == 3 else None)
            t_zgt = ins
            c_zgt = p.op("scalar", lambda e: e.copy(zgT[:], p_trb[:, 0:512]),
                         waits=[(s_t, t_zgt)], inc=s_s)
            w = [(s_a, lda)]
            if t >= 2:
                w.append((s_st1[b], 16 * ((t - 2) // 2 + 1)))
            v_xa = p.op("vector", lambda e, b=b: e.tensor_add(
                x1_s[b][:], x_s[b][:], a_s[b][:]), waits=w, inc=s_v)
            m_xa[t] = v_xa
            for oc in range(8):
                for nb in range(4):
                    w = []
                    if nb == 0:
                        w = [(s_v, s_v.n)]
                        if oc == 0:
                            w.append((s_s, c_zgt))
                    ins = p.op("tensor", lambda e, nb=nb, oc=oc: e.matmul(
                        p_f[:], zgT[:, nb * 128:(nb + 1) * 128],
                        B_T[:, nb * D + oc * 512:nb * D + (oc + 1) * 512],
                        start=(nb == 0), stop=(nb == 3)),
                        waits=w, inc=s_t if nb == 3 else None)
                t_f = ins
                v_acc = p.op("vector", lambda e, b=b, oc=oc: e.tensor_add(
                    x1_s[b][:, oc * 512:(oc + 1) * 512],
                    x1_s[b][:, oc * 512:(oc + 1) * 512], p_f[:]),
                    waits=[(s_t, t_f), (s_v, v_xa)], inc=s_v)
            for i in range(8):
                ins = p.op("vector", lambda e, b=b, i=i: e.bn_stats(
                    stat[:, 6 * i:6 * i + 6], x1_s[b][:, 512 * i:512 * (i + 1)]),
                    inc=s_v if i == 7 else None)
            v_bns2 = ins
            v_bna2 = p.op("vector", lambda e: e.bn_aggr(aggr[:, 0:2], stat[:]),
                          waits=[(s_v, v_bns2)], inc=s_v)
            s_sd2 = p.op("scalar", lambda e: e.activation(
                aggr[:, 2:3], aggr[:, 1:2], AF.Sqrt, bias=eps_s[:, 0:1], scale=1.0),
                waits=[(s_v, v_bna2)], inc=s_s)
            m_sq[t] = s_sd2
            v_rc3 = p.op("vector", lambda e: e.reciprocal(aggr[:, 3:4], aggr[:, 2:3]),
                         waits=[(s_s, s_sd2)], inc=s_v)
            w = [(s_v, v_rc3)]
            if t >= 1:
                w.append((s_st2, 16 * t))
            v_xn1 = p.op("vector", lambda e, b=b: e.tensor_scalar(
                xn1_s[b][:], x1_s[b][:], aggr[:, 0:1], aggr[:, 3:4],
                op0=ALU.subtract, op1=ALU.mult), waits=w, inc=s_v)
            p.op("gpsimd", lambda e, t=t, b=b: e.dma_start(
                out=x1[t * 128:(t + 1) * 128, :], in_=x1_s[b][:]),
                waits=[(s_v, v_xn1)], inc=s_st1[b], inck=16)
            p.op("gpsimd", lambda e, t=t, b=b: e.dma_start(
                out=xn1[t * 128:(t + 1) * 128, :], in_=xn1_s[b][:]),
                inc=s_st2, inck=16)

        p.op("gpsimd", lambda e: e.nop(),
             waits=[(s_st1[0], 16 * ((NT + 1) // 2)), (s_st1[1], 16 * (NT // 2)),
                    (s_st2, 16 * NT)])


def build_ffn0(NT=16):
    nc = bass.Bass("TRN2", target_bir_lowering=False, debug=False)
    Ntok = NT * 128
    xo = nc.dram_tensor("xo", [Ntok, D], F32, kind="ExternalInput").ap()
    a0 = nc.dram_tensor("a0", [Ntok, D], F32, kind="ExternalInput").ap()
    A_w = nc.dram_tensor("A_w", [512, D], F32, kind="ExternalInput").ap()
    B_w = nc.dram_tensor("B_w", [D, 512], F32, kind="ExternalInput").ap()
    Wq_w = nc.dram_tensor("Wq_w", [16, D], F32, kind="ExternalInput").ap()
    keys = nc.dram_tensor("keys", [64, 16], F32, kind="ExternalInput").ap()
    identf = nc.dram_tensor("identf", [128, 128], F32, kind="ExternalInput").ap()
    x1 = nc.dram_tensor("x1", [Ntok, D], F32, kind="ExternalOutput").ap()
    xn1 = nc.dram_tensor("xn1", [Ntok, D], BF16, kind="ExternalOutput").ap()
    p = Prog()
    with ExitStack() as ctx:
        SP = SemPool(ctx, nc)
        emit_ffn0(nc, p, SP, NT, xo, a0, A_w, B_w, Wq_w, keys, identf, x1, xn1)
        block = ctx.enter_context(nc.Block())
        p.emit(block)
    return nc


def run_ffn0(x, attn0, A_w, B_w, Wq_w, keys):
    B, T, _ = x.shape
    ntok = B * T // 8
    nc = build_ffn0(ntok // 128)
    xs = x.reshape(8, ntok, D)
    as_ = attn0.reshape(8, ntok, D)
    in_maps = []
    for r in range(8):
        in_maps.append(dict(
            xo=np.ascontiguousarray(xs[r]), a0=np.ascontiguousarray(as_[r]),
            A_w=A_w, B_w=B_w, Wq_w=Wq_w, keys=keys, identf=_IDENTF))
    res = run_bass_kernel_spmd(nc, in_maps, core_ids=list(range(8)))
    x1 = np.concatenate([res.results[r]["x1"] for r in range(8)], 0).reshape(B, T, D)
    xn1 = np.concatenate([res.results[r]["xn1"] for r in range(8)], 0).reshape(B, T, D)
    return x1, xn1




def emit_attn1(nc, p, SP, NT, xn1, Wq_sel, bq1, Wo, Mm, zT8, ident, attn1,
               pre_waits=()):
    Ntok = NT * 128
    sb, ps = _pfx(nc, "C_")
    with ExitStack() as ctx:
        E = ctx.enter_context
        id_bf = E(sb("id_bf", [128, 128], BF16))
        WqselT = E(sb("WqselT", [128, 32 * 1024], BF16))
        WoF = E(sb("WoF", [128, 8 * D], BF16))
        M_bf = E(sb("M_bf", [128, 8 * 128], BF16))
        zT_bf = E(sb("zT_bf", [128, 8], BF16))
        bq_b = E(sb("bq_b", [128, 1024], F32))
        bq_s = E(sb("bq_s", [1, 1024], F32))
        ones1 = E(sb("ones1", [1, 128], F32))
        z_s = E(sb("z_s", [8, 128], F32))
        z_bf = E(sb("z_bf", [8, 128], BF16))
        xt_bf = E(sb("xt_bf", [128, D], BF16))
        x1T_t = E(sb("x1T_t", [128, D], BF16))
        qb_f = E(sb("qb_f", [128, 1024], F32))
        qm_f = E(sb("qm_f", [128, 1024], F32))
        e1_f = E(sb("e1_f", [128, 1024], F32))
        sig_bf = E(sb("sig_bf", [128, 1024], BF16))
        sigT = E(sb("sigT", [128, 8 * 128], BF16))
        den_f = E(sb("den_f", [128, 8], F32))
        rcp8 = E(sb("rcp8", [128, 8], F32))
        retr = E(sb("retr", [128, 8 * 128], BF16))
        retrT = E(sb("retrT", [128, 8 * 128], BF16))
        ao_s = [E(sb("ao_s0", [128, D], F32))] * 2
        wst = ao_s[0]
        wst_bf = xt_bf

        s_set = SP("C_set")
        s_wst = SP("C_wst")
        s_x = SP("C_x")
        s_st = SP("C_st")
        s_v = SP("s_v")
        s_s = SP("s_s")
        s_t = SP("s_t")
        p.serialize(vector=s_v, scalar=s_s)

        p_trb = E(ps("p_trb", [128, 8 * 128], BF16))
        p_acc = E(ps("p_acc", [128, 8 * 128], F32))
        p_q = E(ps("p_q", [128, 512], F32))
        p_n = E(ps("p_n", [128, 8 * 128], F32))
        p_d = E(ps("p_d", [128, 8], F32))
        p_o = E(ps("p_o", [128, 512], F32))

        p.op("sync", lambda e: e.dma_start(out=id_bf[:], in_=ident),
             waits=list(pre_waits), inc=s_set, inck=16)
        p.op("sync", lambda e: e.dma_start(out=bq_s[:], in_=bq1),
             inc=s_set, inck=16)
        p.op("sync", lambda e: e.dma_start(out=z_s[:], in_=zT8),
             inc=s_set, inck=16)
        ld_all = s_set.n
        v_o1 = p.op("vector", lambda e: e.memset(ones1[:], 1.0), inc=s_v)
        v_zb = p.op("vector", lambda e: e.tensor_copy(z_bf[:], z_s[:]),
                    waits=[(s_set, ld_all)], inc=s_v)
        p.op("tensor", lambda e: e.matmul(p_acc[:, 0:512], ones1[:], bq_s[:, 0:512],
                                          start=True, stop=True),
             waits=[(s_v, v_o1), (s_set, ld_all)])
        t_bq = p.op("tensor", lambda e: e.matmul(p_acc[:, 512:1024], ones1[:],
                                                 bq_s[:, 512:1024], start=True,
                                                 stop=True), inc=s_t)
        c_bq = p.op("scalar", lambda e: e.copy(bq_b[:], p_acc[:]),
                    waits=[(s_t, t_bq)], inc=s_s)
        t_z = p.op("tensor", lambda e: e.transpose(p_trb[:, 0:8], z_bf[:],
                                                   id_bf[0:8, 0:8]),
                   waits=[(s_v, v_zb)], inc=s_t)
        c_z = p.op("scalar", lambda e: e.copy(zT_bf[:], p_trb[:, 0:8]),
                   waits=[(s_t, t_z)], inc=s_s)
        ld_m = p.op("sync", lambda e: e.dma_start(
            out=wst[:, 0:1024].rearrange("p (h e) -> p h e", e=128),
            in_=Mm.rearrange("(h p) e -> p h e", p=128)),
            inc=s_wst, inck=16)
        v_m = p.op("vector", lambda e: e.tensor_copy(M_bf[:], wst[:, 0:1024]),
                   waits=[(s_wst, ld_m)], inc=s_v)
        prev_cp = c_z
        for i in range(8):
            ldw = p.op("sync", lambda e, i=i: e.dma_start(
                out=wst[:], in_=Wq_sel[i * 128:(i + 1) * 128, :]),
                waits=[(s_s, prev_cp), (s_v, v_m)], inc=s_wst, inck=16)
            vcp = p.op("vector", lambda e: e.tensor_copy(wst_bf[:], wst[:]),
                       waits=[(s_wst, ldw)], inc=s_v)
            for grp in range(4):
                for j in range(8):
                    dt = grp * 8 + j
                    wts = [(s_v, vcp), (s_s, s_s.n)] if dt == 0 else (
                        [(s_s, s_s.n)] if j == 0 else ())
                    ins = p.op("tensor", lambda e, dt=dt, j=j: e.transpose(
                        p_trb[:, j * 128:(j + 1) * 128],
                        wst_bf[:, dt * 128:(dt + 1) * 128], id_bf[:]),
                        waits=wts, inc=s_t if j == 7 else None)
                dst = WqselT[:].rearrange("p (dt c) -> p dt c", c=1024)[
                    :, grp * 8:(grp + 1) * 8, i * 128:(i + 1) * 128]
                prev_cp = p.op("scalar", lambda e, dst=dst: e.copy(
                    dst, p_trb[:].rearrange("p (j c) -> p j c", c=128)),
                    waits=[(s_t, ins)], inc=s_s)
        for ot in range(32):
            ldw = p.op("sync", lambda e, ot=ot: e.dma_start(
                out=wst[:], in_=Wo[ot * 128:(ot + 1) * 128, :]),
                waits=[(s_s, prev_cp)], inc=s_wst, inck=16)
            vcp = p.op("vector", lambda e: e.tensor_copy(wst_bf[:], wst[:]),
                       waits=[(s_wst, ldw)], inc=s_v)
            for h in range(8):
                for g in range(4):
                    ins = p.op("tensor", lambda e, h=h, g=g: e.matmul(
                        p_acc[:, h * 128:(h + 1) * 128],
                        wst_bf[:, (4 * h + g) * 128:(4 * h + g + 1) * 128],
                        id_bf[:], start=(g == 0), stop=(g == 3)),
                        waits=[(s_v, vcp), (s_s, prev_cp)] if h == 0 and g == 0 else (),
                        inc=s_t if h == 7 and g == 3 else None)
            dst = WoF[:].rearrange("p (h c) -> p h c", c=D)[
                :, :, ot * 128:(ot + 1) * 128]
            prev_cp = p.op("scalar", lambda e, dst=dst: e.copy(
                dst, p_acc[:].rearrange("p (h c) -> p h c", c=128)),
                waits=[(s_t, ins)], inc=s_s)
        prep_done = prev_cp

        m_trl = {}
        for t in range(NT):
            w = [(s_s, prep_done)] if t == 0 else [(s_t, m_trl[t - 1])]
            ldx = p.op("sync", lambda e, t=t: e.dma_start(
                out=xt_bf[:], in_=xn1[t * 128:(t + 1) * 128, :]),
                waits=w, inc=s_x, inck=16)
            for grp in range(4):
                for j in range(8):
                    dt = grp * 8 + j
                    wts = [(s_x, ldx), (s_s, s_s.n)] if dt == 0 else (
                        [(s_s, s_s.n)] if j == 0 else ())
                    ins = p.op("tensor", lambda e, dt=dt, j=j: e.transpose(
                        p_trb[:, j * 128:(j + 1) * 128],
                        xt_bf[:, dt * 128:(dt + 1) * 128], id_bf[:]),
                        waits=wts, inc=s_t if j == 7 else None)
                dst = x1T_t[:].rearrange("p (dt c) -> p dt c", c=128)[
                    :, grp * 8:(grp + 1) * 8, :]
                cp = p.op("scalar", lambda e, dst=dst: e.copy(
                    dst, p_trb[:].rearrange("p (j c) -> p j c", c=128)),
                    waits=[(s_t, ins)], inc=s_s)
            m_xt = cp
            m_trl[t] = s_t.n
            for qc in range(2):
                for dt in range(32):
                    w = []
                    if dt == 0:
                        w = [(s_s, m_xt), (s_v, s_v.n)]
                    ins = p.op("tensor", lambda e, dt=dt, qc=qc: e.matmul(
                        p_q[:], x1T_t[:, dt * 128:(dt + 1) * 128],
                        WqselT[:, dt * 1024 + qc * 512:dt * 1024 + qc * 512 + 512],
                        start=(dt == 0), stop=(dt == 31)),
                        waits=w, inc=s_t if dt == 31 else None)
                t_q = ins
                v_qb = p.op("vector", lambda e, qc=qc: e.tensor_add(
                    qb_f[:, qc * 512:(qc + 1) * 512], p_q[:, 0:512],
                    bq_b[:, qc * 512:(qc + 1) * 512]),
                    waits=[(s_t, t_q), (s_s, c_bq)], inc=s_v)
            v_qm = p.op("vector", lambda e: e.tensor_scalar(
                qm_f[:], qb_f[:], 0.0, None, op0=ALU.min), inc=s_v)
            s_e1 = p.op("scalar", lambda e: e.activation(e1_f[:], qm_f[:], AF.Exp),
                        waits=[(s_v, v_qm)], inc=s_s)
            v_qr = p.op("vector", lambda e: e.tensor_relu(qm_f[:], qb_f[:]),
                        waits=[(s_s, s_e1)])
            v_sg = p.op("vector", lambda e: e.tensor_add(sig_bf[:], e1_f[:], qm_f[:]),
                        waits=[(s_s, s_e1), (s_v, v_qr)], inc=s_v)
            for j in range(8):
                wts = [(s_v, v_sg), (s_s, s_s.n)] if j == 0 else ()
                ins = p.op("tensor", lambda e, j=j: e.transpose(
                    p_trb[:, j * 128:(j + 1) * 128],
                    sig_bf[:, j * 128:(j + 1) * 128], id_bf[:]),
                    waits=wts, inc=s_t if j == 7 else None)
            c_sT = p.op("scalar", lambda e: e.copy(sigT[:], p_trb[:]),
                        waits=[(s_t, ins)], inc=s_s)
            for h in range(8):
                w = [(s_s, c_sT)] if h == 0 else []
                if h == 0:
                    w.append((s_v, s_v.n))
                p.op("tensor", lambda e, h=h: e.matmul(
                    p_n[:, h * 128:(h + 1) * 128], sigT[:, h * 128:(h + 1) * 128],
                    M_bf[:, h * 128:(h + 1) * 128], start=True, stop=True), waits=w)
                ins = p.op("tensor", lambda e, h=h: e.matmul(
                    p_d[:, h:h + 1], sigT[:, h * 128:(h + 1) * 128],
                    zT_bf[:, h:h + 1], start=True, stop=True),
                    inc=s_t if h == 7 else None)
            t_nd = ins
            v_de = p.op("vector", lambda e: e.tensor_scalar(
                den_f[:], p_d[:], 1e-6, None, op0=ALU.add),
                waits=[(s_t, t_nd)], inc=s_v)
            v_rc = p.op("vector", lambda e: e.reciprocal(rcp8[:], den_f[:]),
                        waits=[(s_v, v_de)], inc=s_v)
            for h in range(8):
                ins = p.op("vector", lambda e, h=h: e.tensor_scalar(
                    retr[:, h * 128:(h + 1) * 128], p_n[:, h * 128:(h + 1) * 128],
                    rcp8[:, h:h + 1], None, op0=ALU.mult),
                    waits=[(s_v, v_rc)] if h == 0 else (),
                    inc=s_v if h == 7 else None)
            v_re = ins
            for j in range(8):
                wts = [(s_v, v_re), (s_s, s_s.n)] if j == 0 else ()
                ins = p.op("tensor", lambda e, j=j: e.transpose(
                    p_trb[:, j * 128:(j + 1) * 128],
                    retr[:, j * 128:(j + 1) * 128], id_bf[:]),
                    waits=wts, inc=s_t if j == 7 else None)
            c_rT = p.op("scalar", lambda e: e.copy(retrT[:], p_trb[:]),
                        waits=[(s_t, ins)], inc=s_s)
            b = t % 2
            for oc in range(8):
                for h in range(8):
                    w = []
                    if h == 0:
                        w = [(s_s, c_rT)]
                        if oc == 0 and t >= 2:
                            w.append((s_st, 16 * (t - 1)))
                        if oc > 0:
                            w.append((s_s, s_s.n))
                    ins = p.op("tensor", lambda e, oc=oc, h=h: e.matmul(
                        p_o[:], retrT[:, h * 128:(h + 1) * 128],
                        WoF[:, h * D + oc * 512:h * D + (oc + 1) * 512],
                        start=(h == 0), stop=(h == 7)),
                        waits=w, inc=s_t if h == 7 else None)
                w2 = [(s_t, ins)]
                if oc == 0 and t >= 1:
                    w2.append((s_st, 16 * t))
                c_o = p.op("scalar", lambda e, b=b, oc=oc: e.copy(
                    ao_s[b][:, oc * 512:(oc + 1) * 512], p_o[:]),
                    waits=w2, inc=s_s)
            p.op("gpsimd", lambda e, t=t, b=b: e.dma_start(
                out=attn1[t * 128:(t + 1) * 128, :], in_=ao_s[b][:]),
                waits=[(s_s, c_o)], inc=s_st, inck=16)

        p.op("gpsimd", lambda e: e.nop(), waits=[(s_st, 16 * NT)])


def build_attn1(NT=16):
    nc = bass.Bass("TRN2", target_bir_lowering=False, debug=False)
    Ntok = NT * 128
    xn1 = nc.dram_tensor("xn1", [Ntok, D], BF16, kind="ExternalInput").ap()
    Wq_sel = nc.dram_tensor("Wq_sel", [1024, D], F32, kind="ExternalInput").ap()
    bq1 = nc.dram_tensor("bq1", [1, 1024], F32, kind="ExternalInput").ap()
    Wo = nc.dram_tensor("Wo", [D, D], F32, kind="ExternalInput").ap()
    Mm = nc.dram_tensor("Mm", [1024, 128], F32, kind="ExternalInput").ap()
    zT8 = nc.dram_tensor("zT8", [8, 128], F32, kind="ExternalInput").ap()
    ident = nc.dram_tensor("ident", [128, 128], BF16, kind="ExternalInput").ap()
    attn1 = nc.dram_tensor("attn1", [Ntok, D], F32, kind="ExternalOutput").ap()
    p = Prog()
    with ExitStack() as ctx:
        SP = SemPool(ctx, nc)
        emit_attn1(nc, p, SP, NT, xn1, Wq_sel, bq1, Wo, Mm, zT8, ident, attn1)
        block = ctx.enter_context(nc.Block())
        p.emit(block)
    return nc


def run_attn1(xn1, Wq, bq, Wo, M, z):
    Ntok_all = xn1.shape[0] * xn1.shape[1]
    ntok = Ntok_all // 8
    nc = build_attn1(ntok // 128)
    sel = np.concatenate([np.arange(512 * h, 512 * h + 128) for h in range(8)])
    Wq_sel = np.ascontiguousarray(Wq[sel])
    bq_sel = np.ascontiguousarray(bq[sel]).reshape(1, 1024)
    xs = xn1.reshape(8, ntok, D)
    in_maps = []
    for r in range(8):
        in_maps.append(dict(
            xn1=np.ascontiguousarray(xs[r]).astype(BF, copy=False),
            Wq_sel=Wq_sel, bq1=bq_sel, Wo=Wo,
            Mm=np.ascontiguousarray(M.reshape(1024, 128)),
            zT8=np.ascontiguousarray(z), ident=_IDENT))
    res = run_bass_kernel_spmd(nc, in_maps, core_ids=list(range(8)))
    out = np.concatenate([res.results[r]["attn1"] for r in range(8)], 0)
    return out.reshape(xn1.shape[0], xn1.shape[1], D)




def emit_ffn1(nc, p, SP, NT, xn1, x1o, a1o, qwgu, qzgu, scgu, qwdn, qzdn,
              scdn, ident, sel, out, hbuf, wdnbuf, pre_waits=()):
    Ntok = NT * 128
    TCH = 512 if Ntok % 512 == 0 else Ntok
    NTCH = Ntok // TCH
    TC = 1024 if Ntok % 1024 == 0 else Ntok
    NTC = Ntok // TC
    INTER = 11008
    NCH = 43
    NIB = 86
    sb, ps = _pfx(nc, "D_")
    with ExitStack() as ctx:
        E = ctx.enter_context
        id_bf = E(sb("id_bf", [128, 128], BF16))
        ones1 = E(sb("ones1", [1, 128], F32))
        s_set = SP("D_set")
        s_x = SP("D_x")
        s_qw = SP("D_qw")
        s_scp = [SP(f"D_scp{i}") for i in range(2)]
        s_qw2 = [SP(f"D_qw2{i}") for i in range(2)]
        Sh = [SP(f"D_h{i}") for i in range(2)]
        s_hl = SP("D_hl")
        s_fin = SP("D_fin")
        s_st = SP("D_st")
        s_v = SP("s_v")
        s_s = SP("s_s")
        s_t = SP("s_t")
        p.serialize(vector=s_v, scalar=s_s)

        with ExitStack() as d1:
            E1 = d1.enter_context
            sel_s = E1(sb("sel_s", [32, 32 * 128], BF16))
            p.op("sync", lambda e: e.dma_start(out=id_bf[:], in_=ident),
                 waits=list(pre_waits), inc=s_set, inck=16)
            p.op("sync", lambda e: e.dma_start(out=sel_s[:], in_=sel), inc=s_set, inck=16)
            ld_set = s_set.n
            v_o1 = p.op("vector", lambda e: e.memset(ones1[:], 1.0), waits=[(s_set, ld_set)])
            xn1T = E1(sb("xn1T", [128, 32 * Ntok], BF16))
            xt_bf = E1(sb("xt_bf", [128, D], BF16))
            qzc_s = E1(sb("qzc_s", [32, 32], I32))
            Wg = E1(sb("Wg", [128, 32 * 256], BF16))
            Wu = E1(sb("Wu", [128, 32 * 256], BF16))
            qwst = [E1(sb(f"qwst{i}", [128, 1024], I32)) for i in range(2)]
            wi32 = E1(sb("wi32", [128, 8 * 256], I32))
            scch = [E1(sb(f"scch{i}", [32, 256], F32)) for i in range(2)]
            scbf = [E1(sb(f"scbf{i}", [32, 256], BF16)) for i in range(2)]
            zi_s = E1(sb("zi_s", [32, 256], I32))
            zbf_s = E1(sb("zbf_s", [32, 256], BF16))
            wd_f = E1(sb("wd_f", [128, 512], F32))
            g_act = E1(sb("g_act", [128, TCH], F32))
            hT_bf = [E1(sb(f"hT_bf{i}", [128, TCH], BF16)) for i in range(2)]
            p_trb = E1(ps("p_trb", [128, 8 * 128], BF16))
            p_sb = [E1(ps(f"p_sb{i}", [128, 256], F32)) for i in range(2)]
            p_zb = [E1(ps(f"p_zb{i}", [128, 256], F32)) for i in range(2)]
            p_g = E1(ps("p_g", [128, TCH], F32))
            p_u = E1(ps("p_u", [128, TCH], F32))

            m_trl = {}
            for t in range(NT):
                w = [] if t == 0 else [(s_t, m_trl[t - 1])]
                ldx = p.op("sync", lambda e, t=t: e.dma_start(
                    out=xt_bf[:], in_=xn1[t * 128:(t + 1) * 128, :]),
                    waits=w, inc=s_x, inck=16)
                for grp in range(4):
                    for j in range(8):
                        dt = grp * 8 + j
                        wts = [(s_x, ldx), (s_set, ld_set), (s_s, s_s.n)] if dt == 0 else (
                            [(s_s, s_s.n)] if j == 0 else ())
                        ins = p.op("tensor", lambda e, dt=dt, j=j: e.transpose(
                            p_trb[:, j * 128:(j + 1) * 128],
                            xt_bf[:, dt * 128:(dt + 1) * 128], id_bf[:]),
                            waits=wts, inc=s_t if j == 7 else None)
                    dst = xn1T[:].rearrange("p (dt c) -> p dt c", c=Ntok)[
                        :, grp * 8:(grp + 1) * 8, t * 128:(t + 1) * 128]
                    cp = p.op("scalar", lambda e, dst=dst: e.copy(
                        dst, p_trb[:].rearrange("p (j c) -> p j c", c=128)),
                        waits=[(s_t, ins)])
                m_trl[t] = s_t.n
            m_xnt = s_s.n

            hs_cnt = [0, 0]
            dqmult = [None, None]
            for c in range(NCH):
                halves = []
                for half, Wbuf in enumerate([Wg, Wu]):
                    cb = (2 * c + half) % 2
                    pc0 = half * 1376 + c * 32
                    ic0 = half * INTER + c * 256
                    w = []
                    if c > 0 or half > 0:
                        w = [(s_v, m_unp)]
                    for dt in range(32):
                        ldw = p.op("sync", lambda e, dt=dt, pc0=pc0, cb=cb: e.dma_start(
                            out=qwst[cb][:, dt * 32:(dt + 1) * 32],
                            in_=qwgu[dt * 128:(dt + 1) * 128, pc0:pc0 + 32]),
                            waits=w if dt == 0 else (), inc=s_qw, inck=16)
                    p.op("sync", lambda e, ic0=ic0, cb=cb: e.dma_start(
                        out=scch[cb][:], in_=scgu[:, ic0:ic0 + 256]),
                        inc=s_scp[cb], inck=16)
                    ldsc = p.op("sync", lambda e, pc0=pc0: e.dma_start(
                        out=qzc_s[:], in_=qzgu[:, pc0:pc0 + 32]),
                        inc=s_scp[cb], inck=16)
                    w = [(s_scp[cb], ldsc)]
                    p.op("vector", lambda e, cb=cb: e.tensor_copy(
                        scbf[cb][:], scch[cb][:]), waits=w)
                    for nib in range(8):
                        dst = zi_s[:].rearrange("p (j n) -> p j n", n=8)[:, :, nib]
                        ins = p.op("vector", lambda e, nib=nib, dst=dst:
                                   e.tensor_scalar(dst, qzc_s[:],
                                                   4 * nib, 15,
                                                   op0=ALU.logical_shift_right,
                                                   op1=ALU.bitwise_and),
                                   waits=w if nib == 0 else ())
                    m_prep1 = p.op("vector", lambda e: e.tensor_copy(zbf_s[:], zi_s[:]))
                    for rnd in range(4):
                        w = [(s_qw, s_qw.n)]
                        if c > 0 or half > 0 or rnd > 0:
                            w.append((s_v, s_v.n))
                        for nib in range(8):
                            src = qwst[cb][:].rearrange(
                                "p (dt j) -> p dt j", j=32)[:, rnd * 8:(rnd + 1) * 8, :]
                            dst = wi32[:].rearrange(
                                "p (dt j n) -> p dt j n", j=32, n=8)[:, :, :, nib]
                            ins = p.op("vector", lambda e, src=src, dst=dst, nib=nib:
                                       e.tensor_scalar(dst, src, 4 * nib, 15,
                                                       op0=ALU.logical_shift_right,
                                                       op1=ALU.bitwise_and),
                                       waits=w if nib == 0 else ())
                        m_unp = ins
                        for k in range(8):
                            dt = rnd * 8 + k
                            par = dt % 2
                            w0 = [(s_v, m_prep1)]
                            if dqmult[par]:
                                w0.append((s_v, dqmult[par]))
                            p.op("tensor", lambda e, dt=dt, par=par: e.matmul(
                                p_zb[par][:],
                                sel_s[:, dt * 128:(dt + 1) * 128],
                                zbf_s[:], start=True, stop=True), waits=w0)
                            t_sb = p.op("tensor", lambda e, dt=dt, cb=cb, par=par: e.matmul(
                                p_sb[par][:],
                                sel_s[:, dt * 128:(dt + 1) * 128],
                                scbf[cb][:], start=True, stop=True), inc=s_t)
                            p.op("vector",
                                 lambda e, k=k, par=par: e.tensor_tensor(
                                     wd_f[:, par * 256:(par + 1) * 256],
                                     wi32[:, k * 256:(k + 1) * 256],
                                     p_zb[par][:],
                                     op=ALU.subtract),
                                 waits=[(s_t, t_sb)])
                            dqmult[par] = p.op(
                                "vector",
                                lambda e, dt=dt, Wbuf=Wbuf, par=par: e.tensor_tensor(
                                    Wbuf[:, dt * 256:(dt + 1) * 256],
                                    wd_f[:, par * 256:(par + 1) * 256],
                                    p_sb[par][:], op=ALU.mult))
                            m_unp = dqmult[par]
                    halves.append(m_unp)
                m_wg, m_wu = halves[0], halves[1]

                for tch in range(NTCH):
                    for isub in range(2):
                        gi = c * 256 + isub * 128
                        hb = (tch * 2 + isub) % 2
                        w = [(s_v, s_v.n), (s_s, s_s.n)]
                        for dt in range(32):
                            ins = p.op("tensor", lambda e, dt=dt, isub=isub, tch=tch:
                                       e.matmul(
                                           p_g[:],
                                           Wg[:, dt * 256 + isub * 128:dt * 256 + isub * 128 + 128],
                                           xn1T[:, dt * Ntok + tch * TCH:dt * Ntok + (tch + 1) * TCH],
                                           start=(dt == 0), stop=(dt == 31)),
                                       waits=w if dt == 0 else ())
                        for dt in range(32):
                            ins = p.op("tensor", lambda e, dt=dt, isub=isub, tch=tch:
                                       e.matmul(
                                           p_u[:],
                                           Wu[:, dt * 256 + isub * 128:dt * 256 + isub * 128 + 128],
                                           xn1T[:, dt * Ntok + tch * TCH:dt * Ntok + (tch + 1) * TCH],
                                           start=(dt == 0), stop=(dt == 31)),
                                       inc=s_t if dt == 31 else None)
                        t_gu = ins
                        s_si = p.op("scalar", lambda e: e.activation(
                            g_act[:], p_g[:], AF.Silu), waits=[(s_t, t_gu)])
                        w2 = [(s_s, s_si)]
                        if hs_cnt[hb] > 0:
                            w2.append((Sh[hb], 16 * hs_cnt[hb]))
                        m_h = p.op("vector", lambda e, hb=hb: e.tensor_tensor(
                            hT_bf[hb][:], g_act[:], p_u[:], op=ALU.mult), waits=w2)
                        p.op("gpsimd", lambda e, gi=gi, tch=tch, hb=hb: e.dma_start(
                            out=hbuf[gi:gi + 128, tch * TCH:(tch + 1) * TCH],
                            in_=hT_bf[hb][:]),
                            waits=[(s_v, m_h)], inc=Sh[hb], inck=16)
                        hs_cnt[hb] += 1
            d1_t, d1_v, d1_s = s_t.n, s_v.n, s_s.n

        with ExitStack() as d2:
            E2 = d2.enter_context
            IG = 4
            C_sb = E2(sb("C_sb", [128, (TC // 128) * D], F32))
            Wd = [E2(sb(f"Wd{i}", [128, D], BF16)) for i in range(IG)]
            hT_l = [E2(sb(f"hT_l{i}", [128, TC], BF16)) for i in range(IG)]
            qwst2 = [E2(sb(f"qwst2{i}", [128, 128], I32)) for i in range(2)]
            wi2 = E2(sb("wi2", [128, 1024], I32))
            wd_f2 = E2(sb("wd_f2", [128, 1024], F32))
            scr_s = [E2(sb(f"scr_s{i}", [1, 1024], F32)) for i in range(2)]
            scr_bf = [E2(sb(f"scr_bf{i}", [1, 1024], BF16)) for i in range(2)]
            qzr_s = [E2(sb(f"qzr_s{i}", [1, 128], I32)) for i in range(2)]
            zr_i = E2(sb("zr_i", [1, 1024], I32))
            zr_bf = [E2(sb(f"zr_bf{i}", [1, 1024], BF16)) for i in range(2)]
            ones_bf = E2(sb("ones_bf", [1, 128], BF16))
            fx_s = E2(sb("fx_s", [128, 512], F32))
            fa_s = E2(sb("fa_s", [128, 512], F32))
            fo_s = fx_s
            p_zb2 = E2(ps("p_zb2", [128, 1024], F32))
            p_sb2 = E2(ps("p_sb2", [128, 1024], F32))
            p_d = E2(ps("p_d", [128, 512], F32))

            ngrp = (NIB + IG - 1) // IG
            m_dmm = {}
            dq2cons = [None, None]
            dq2mult = [None, None]
            ld2q = {}
            s_dz = [SP(f"D_z{i}") for i in range(2)]
            s_ds = [SP(f"D_s{i}") for i in range(2)]
            hl_cnt = 0
            qw2_cnt = 0
            wb_cnt = 0
            fin_cnt = 0
            st_cnt = 0
            s_wb = SP("D_wb")
            for tc in range(NTC):
                for igi in range(ngrp):
                    ibs = list(range(igi * IG, min((igi + 1) * IG, NIB)))
                    for sl, ib in enumerate(ibs):
                        cb = ib % 2
                        w = []
                        if tc == 0 and igi == 0 and sl == 0:
                            w = [(Sh[0], 16 * hs_cnt[0]), (Sh[1], 16 * hs_cnt[1]),
                                 (s_t, d1_t), (s_v, d1_v), (s_s, d1_s)]
                        if sl == 0 and (tc, igi) != (0, 0):
                            w.append((s_t, m_dmm.get((tc, igi - 1),
                                                     m_dmm.get((tc - 1, ngrp - 1), 0))))
                        if qw2_cnt > 0:
                            w.append((s_v, m_wd))
                        ldh = p.op("sync", lambda e, ib=ib, tc=tc, sl=sl: e.dma_start(
                            out=hT_l[sl][:], in_=hbuf[ib * 128:(ib + 1) * 128,
                                                      tc * TC:(tc + 1) * TC]),
                            waits=w, inc=s_hl, inck=16)
                        hl_cnt += 1
                        if tc > 0:
                            m_wd = s_v.n
                            ldwd = p.op("sync", lambda e, ib=ib, sl=sl: e.dma_start(
                                out=Wd[sl][:], in_=wdnbuf[ib * 128:(ib + 1) * 128, :]),
                                inc=s_wb, inck=16)
                            wb_cnt += 1
                            continue
                        if qw2_cnt == 0:
                            p.op("vector", lambda e: e.memset(ones_bf[:], 1.0))
                        qw2_cnt += 1
                        def emit_loads(oh, ib=ib):
                            par = oh % 2
                            w0 = [(s_wb, 16 * wb_cnt)]
                            w0.append((s_v, dq2cons[par] if dq2cons[par] is not None
                                       else s_v.n))
                            ldq = p.op("sync", lambda e, ib=ib, par=par, oh=oh: e.dma_start(
                                out=qwst2[par][:],
                                in_=qwdn[ib * 128:(ib + 1) * 128, oh * 128:(oh + 1) * 128]),
                                waits=w0, inc=s_qw2[par], inck=16)
                            ldqz = p.op("sync", lambda e, ib=ib, par=par, oh=oh: e.dma_start(
                                out=qzr_s[par][:],
                                in_=qzdn[ib:ib + 1, oh * 128:(oh + 1) * 128]),
                                inc=s_dz[par], inck=16)
                            ldsc = p.op("sync", lambda e, ib=ib, par=par, oh=oh: e.dma_start(
                                out=scr_s[par][:],
                                in_=scdn[ib:ib + 1, oh * 1024:(oh + 1) * 1024]),
                                inc=s_ds[par], inck=16)
                            ld2q[oh] = (ldq, ldqz, ldsc)
                        emit_loads(0)
                        emit_loads(1)
                        for oh in range(4):
                            par = oh % 2
                            ldq, ldqz, ldsc = ld2q[oh]
                            p.op("vector", lambda e, par=par: e.tensor_copy(
                                scr_bf[par][:], scr_s[par][:]),
                                waits=[(s_ds[par], ldsc), (s_dz[par], ldqz)])
                            for nib in range(8):
                                dst = zr_i[:].rearrange("p (j n) -> p j n", n=8)[:, :, nib]
                                p.op("vector", lambda e, nib=nib, dst=dst, par=par:
                                     e.tensor_scalar(
                                         dst, qzr_s[par][:], 4 * nib, 15,
                                         op0=ALU.logical_shift_right,
                                         op1=ALU.bitwise_and))
                            m_prep2 = p.op("vector", lambda e, par=par: e.tensor_copy(
                                zr_bf[par][:], zr_i[:]))
                            for nib in range(8):
                                src = qwst2[par][:]
                                dst = wi2[:, 0:1024].rearrange(
                                    "p (j n) -> p j n", n=8)[:, :, nib]
                                p.op("vector", lambda e, src=src, dst=dst, nib=nib:
                                     e.tensor_scalar(dst, src, 4 * nib, 15,
                                                     op0=ALU.logical_shift_right,
                                                     op1=ALU.bitwise_and),
                                     waits=[(s_qw2[par], ldq)] if nib == 0 else ())
                            for oq in range(2):
                                pq = oq
                                w1 = [(s_v, m_prep2)]
                                if dq2mult[pq] is not None:
                                    w1.append((s_v, dq2mult[pq]))
                                t_zb = p.op("tensor", lambda e, oq=oq, par=par, pq=pq:
                                            e.matmul(
                                                p_zb2[:, pq * 512:(pq + 1) * 512],
                                                ones_bf[:],
                                                zr_bf[par][:, oq * 512:(oq + 1) * 512],
                                                start=True, stop=True), waits=w1)
                                t_sb = p.op("tensor", lambda e, oq=oq, par=par, pq=pq:
                                            e.matmul(
                                                p_sb2[:, pq * 512:(pq + 1) * 512],
                                                ones_bf[:],
                                                scr_bf[par][:, oq * 512:(oq + 1) * 512],
                                                start=True, stop=True), inc=s_t)
                                p.op("vector", lambda e, oq=oq, pq=pq: e.tensor_tensor(
                                    wd_f2[:, pq * 512:(pq + 1) * 512],
                                    wi2[:, oq * 512:(oq + 1) * 512],
                                    p_zb2[:, pq * 512:(pq + 1) * 512],
                                    op=ALU.subtract),
                                    waits=[(s_t, t_sb)])
                                m_wd = p.op("vector",
                                            lambda e, sl=sl, oh=oh, oq=oq, pq=pq:
                                            e.tensor_tensor(
                                                Wd[sl][:, oh * 1024 + oq * 512:
                                                       oh * 1024 + (oq + 1) * 512],
                                                wd_f2[:, pq * 512:(pq + 1) * 512],
                                                p_sb2[:, pq * 512:(pq + 1) * 512],
                                                op=ALU.mult))
                                dq2mult[pq] = m_wd
                            dq2cons[par] = m_wd
                            if oh + 2 < 4:
                                emit_loads(oh + 2)
                        p.op("gpsimd", lambda e, ib=ib, sl=sl: e.dma_start(
                            out=wdnbuf[ib * 128:(ib + 1) * 128, :], in_=Wd[sl][:]),
                            waits=[(s_v, m_wd)], inc=s_wb, inck=16)
                        wb_cnt += 1
                    for tt in range(TC // 128):
                        for oc in range(8):
                            w = [(s_v, s_v.n), (s_hl, 16 * hl_cnt)]
                            if tc > 0:
                                w.append((s_wb, 16 * wb_cnt))
                            for sl in range(len(ibs)):
                                last = sl == len(ibs) - 1
                                ins = p.op("tensor",
                                           lambda e, sl=sl, tt=tt, oc=oc, last=last: e.matmul(
                                               p_d[:], hT_l[sl][:, tt * 128:(tt + 1) * 128],
                                               Wd[sl][:, oc * 512:(oc + 1) * 512],
                                               start=(sl == 0), stop=last),
                                           waits=w if sl == 0 else (),
                                           inc=s_t if last else None)
                            if igi == 0:
                                p.op("vector", lambda e, tt=tt, oc=oc: e.tensor_copy(
                                    C_sb[:, tt * D + oc * 512:tt * D + (oc + 1) * 512],
                                    p_d[:]), waits=[(s_t, ins)])
                            else:
                                p.op("vector", lambda e, tt=tt, oc=oc: e.tensor_add(
                                    C_sb[:, tt * D + oc * 512:tt * D + (oc + 1) * 512],
                                    C_sb[:, tt * D + oc * 512:tt * D + (oc + 1) * 512],
                                    p_d[:]), waits=[(s_t, ins)])
                    m_dmm[(tc, igi)] = s_t.n
                for tt in range(TC // 128):
                    gt = tc * (TC // 128) + tt
                    for qh in range(8):
                        r0 = gt * 128
                        cc = qh * 512
                        w = []
                        if fin_cnt > 0:
                            w = [(s_v, s_v.n)]
                        if st_cnt > 0:
                            w.append((s_st, 16 * st_cnt))
                        p.op("sync", lambda e, r0=r0, cc=cc: e.dma_start(
                            out=fx_s[:], in_=x1o[r0:r0 + 128, cc:cc + 512]),
                            waits=w, inc=s_fin, inck=16)
                        p.op("sync", lambda e, r0=r0, cc=cc: e.dma_start(
                            out=fa_s[:], in_=a1o[r0:r0 + 128, cc:cc + 512]),
                            inc=s_fin, inck=16)
                        fin_cnt += 2
                        w2 = [(s_fin, 16 * fin_cnt)]
                        p.op("vector", lambda e, tt=tt, cc=cc: e.tensor_add(
                            fo_s[:], fx_s[:], C_sb[:, tt * D + cc:tt * D + cc + 512]),
                            waits=w2)
                        m_fo = p.op("vector", lambda e: e.tensor_add(
                            fo_s[:], fo_s[:], fa_s[:]))
                        p.op("gpsimd", lambda e, r0=r0, cc=cc: e.dma_start(
                            out=out[r0:r0 + 128, cc:cc + 512], in_=fo_s[:]),
                            waits=[(s_v, m_fo)], inc=s_st, inck=16)
                        st_cnt += 1

            p.op("gpsimd", lambda e: e.nop(), waits=[(s_st, 16 * st_cnt)])


def build_ffn1(NT=16):
    nc = bass.Bass("TRN2", target_bir_lowering=False, debug=False)
    Ntok = NT * 128
    INTER = 11008
    xn1 = nc.dram_tensor("xn1", [Ntok, D], BF16, kind="ExternalInput").ap()
    x1o = nc.dram_tensor("x1o", [Ntok, D], F32, kind="ExternalInput").ap()
    a1o = nc.dram_tensor("a1o", [Ntok, D], F32, kind="ExternalInput").ap()
    qwgu = nc.dram_tensor("qwgu", [D, 2752], I32, kind="ExternalInput").ap()
    qzgu = nc.dram_tensor("qzgu", [32, 2752], I32, kind="ExternalInput").ap()
    scgu = nc.dram_tensor("scgu", [32, 2 * INTER], F32, kind="ExternalInput").ap()
    qwdn = nc.dram_tensor("qwdn", [INTER, 512], I32, kind="ExternalInput").ap()
    qzdn = nc.dram_tensor("qzdn", [86, 512], I32, kind="ExternalInput").ap()
    scdn = nc.dram_tensor("scdn", [86, D], F32, kind="ExternalInput").ap()
    ident = nc.dram_tensor("ident", [128, 128], BF16, kind="ExternalInput").ap()
    ind = nc.dram_tensor("ind", [128, 32 * 32], BF16, kind="ExternalInput").ap()
    sel = nc.dram_tensor("sel", [32, 32 * 128], BF16, kind="ExternalInput").ap()
    out = nc.dram_tensor("out", [Ntok, D], F32, kind="ExternalOutput").ap()
    hbuf = nc.dram_tensor("hbuf", [INTER, Ntok], BF16, kind="Internal").ap()
    wdnbuf = nc.dram_tensor("wdnbuf", [INTER, D], BF16, kind="Internal").ap()
    p = Prog()
    with ExitStack() as ctx:
        SP = SemPool(ctx, nc)
        emit_ffn1(nc, p, SP, NT, xn1, x1o, a1o, qwgu, qzgu, scgu, qwdn, qzdn,
                  scdn, ident, sel, out, hbuf, wdnbuf)
        block = ctx.enter_context(nc.Block())
        p.emit(block)
    return nc


_IND = np.zeros((128, 32 * 32), dtype=BF)
for _dt in range(32):
    _IND[:, _dt * 32 + _dt] = 1
_SEL = np.zeros((32, 32 * 128), dtype=BF)
for _g in range(32):
    _SEL[_g, _g * 128:(_g + 1) * 128] = 1


def run_ffn1(xn1, x1, attn1, qw_gu, qz_gu, sc_gu, qw_dn, qz_dn, sc_dn):
    B, T, _ = xn1.shape
    ntok = B * T // 8
    nc = build_ffn1(ntok // 128)
    xs = xn1.reshape(8, ntok, D)
    x1s = x1.reshape(8, ntok, D)
    a1s = attn1.reshape(8, ntok, D)
    in_maps = []
    for r in range(8):
        in_maps.append(dict(
            xn1=np.ascontiguousarray(xs[r]).astype(BF, copy=False),
            x1o=np.ascontiguousarray(x1s[r]), a1o=np.ascontiguousarray(a1s[r]),
            qwgu=qw_gu, qzgu=qz_gu, scgu=sc_gu, qwdn=qw_dn, qzdn=qz_dn,
            scdn=sc_dn, ident=_IDENT, ind=_IND, sel=_SEL))
    res = run_bass_kernel_spmd(nc, in_maps, core_ids=list(range(8)))
    o = np.concatenate([res.results[r]["out"] for r in range(8)], 0)
    return o.reshape(B, T, D)



CC_GROUPS = [[0, 1, 2, 3], [4, 5, 6, 7]]


def build_fused(T=8192):
    NTOK = T // 4
    NT = NTOK // 128
    INTER = 11008
    nc = bass.Bass("TRN2", target_bir_lowering=False, debug=False)
    t_in = lambda n, sh, dt: nc.dram_tensor(n, sh, dt, kind="ExternalInput").ap()
    xb = t_in("xb", [T, D], F32)
    bvq8 = t_in("bvq8", [8, 128], F32)
    bvk8 = t_in("bvk8", [8, 128], F32)
    bvv1 = t_in("bvv1", [1, HD0], F32)
    ident = t_in("ident", [128, 128], BF16)
    identf = t_in("identf", [128, 128], F32)
    triu = t_in("triu", [128, 128], BF16)
    xo = t_in("xo", [NTOK, D], F32)
    A_w = t_in("A_w", [512, D], F32)
    B_w = t_in("B_w", [D, 512], F32)
    Wq_w = t_in("Wq_w", [16, D], F32)
    keys = t_in("keys", [64, 16], F32)
    a0idx = t_in("a0idx", [128, 4 * NT], I32)
    Wq_sel = t_in("Wq_sel", [1024, D], F32)
    bq1 = t_in("bq1", [1, 1024], F32)
    Wo = t_in("Wo", [D, D], F32)
    Mm = t_in("Mm", [1024, 128], F32)
    zT8 = t_in("zT8", [8, 128], F32)
    qwgu = t_in("qwgu", [D, 2752], I32)
    qzgu = t_in("qzgu", [32, 2752], I32)
    scgu = t_in("scgu", [32, 2 * INTER], F32)
    qwdn = t_in("qwdn", [INTER, 512], I32)
    qzdn = t_in("qzdn", [86, 512], I32)
    scdn = t_in("scdn", [86, D], F32)
    sel = t_in("sel", [32, 32 * 128], BF16)
    out = nc.dram_tensor("out", [NTOK, D], F32, kind="ExternalOutput").ap()
    vbuf = nc.dram_tensor("vbuf", [T, HD0], BF16, kind="Internal").ap()
    a0loc = nc.dram_tensor("a0loc", [T, HD0], F32, kind="Internal").ap()
    a0g = nc.dram_tensor("a0g", [4 * T, HD0], F32, kind="Internal").ap()
    x1b = nc.dram_tensor("x1b", [NTOK, D], F32, kind="Internal").ap()
    xn1b = nc.dram_tensor("xn1b", [NTOK, D], BF16, kind="Internal").ap()
    a1b = nc.dram_tensor("a1b", [NTOK, D], F32, kind="Internal").ap()
    hbuf = nc.dram_tensor("hbuf", [INTER, NTOK], BF16, kind="Internal").ap()
    wdnbuf = nc.dram_tensor("wdnbuf", [INTER, D], BF16, kind="Internal").ap()

    p = Prog()
    with ExitStack() as ctx:
        SP = SemPool(ctx, nc)
        NC_A = emit_attn0(nc, p, SP, T, xb, bvq8, bvk8, bvv1, ident, identf,
                          triu, a0loc, vbuf)
        s_cc = SP("s_cc")
        p.op("gpsimd", lambda e: e.collective_compute(
            "AllGather", ALU.bypass, CC_GROUPS, [a0loc.opt()], [a0g.opt()]),
            waits=[(SP("A_a0"), 16 * NC_A), (SP("A_a1"), 16 * NC_A)],
            inc=s_cc, inck=1)
        p.barrier(SP, extra=[(s_cc, 1)])
        emit_ffn0(nc, p, SP, NT, xo, None, A_w, B_w, Wq_w, keys, identf,
                  x1b, xn1b, a0g=a0g, a0idx=a0idx)
        p.barrier(SP, extra=[(SP("B_st10"), 16 * ((NT + 1) // 2)),
                             (SP("B_st11"), 16 * (NT // 2)),
                             (SP("B_st2"), 16 * NT)])
        emit_attn1(nc, p, SP, NT, xn1b, Wq_sel, bq1, Wo, Mm, zT8, ident, a1b)
        p.barrier(SP, extra=[(SP("C_st"), 16 * NT)])
        emit_ffn1(nc, p, SP, NT, xn1b, x1b, a1b, qwgu, qzgu, scgu, qwdn, qzdn,
                  scdn, ident, sel, out, hbuf, wdnbuf)
        block = ctx.enter_context(nc.Block())
        p.emit(block)
    return nc


def _fused_in_maps(inputs, T):
    x = np.asarray(inputs["L0_attn_x"])
    NTOK = T // 4
    NT = NTOK // 128
    bv_q = np.asarray(inputs["L0_attn_bv_q"])
    bv_k = np.asarray(inputs["L0_attn_bv_k"])
    bv_v = np.asarray(inputs["L0_attn_bv_v"])
    Wq = np.asarray(inputs["L1_attn_Wq"])
    bq = np.asarray(inputs["L1_attn_bq"])
    selq = np.concatenate([np.arange(512 * h, 512 * h + 128) for h in range(8)])
    Wq_sel = np.ascontiguousarray(Wq[selq])
    bq_sel = np.ascontiguousarray(bq[selq]).reshape(1, 1024)
    xs = x.reshape(8, NTOK, D)
    in_maps = []
    for r in range(8):
        b, h = r // 4, r % 4
        c = r % 4
        idx = np.empty((128, 4 * NT), np.int32)
        for t in range(NT):
            for hh in range(4):
                idx[:, t * 4 + hh] = hh * T + c * NTOK + t * 128 + np.arange(128)
        in_maps.append(dict(
            xb=np.ascontiguousarray(np.roll(x[b], -HD0 * h, axis=1)),
            bvq8=np.ascontiguousarray(bv_q[h].reshape(8, 128)),
            bvk8=np.ascontiguousarray(bv_k[h].reshape(8, 128)),
            bvv1=np.ascontiguousarray(bv_v[h].reshape(1, HD0)),
            ident=_IDENT, identf=_IDENTF, triu=_TRIU,
            xo=np.ascontiguousarray(xs[r]),
            A_w=np.asarray(inputs["L0_ffn_A_w"]),
            B_w=np.asarray(inputs["L0_ffn_B_w"]),
            Wq_w=np.asarray(inputs["L0_ffn_Wq_w"]),
            keys=np.asarray(inputs["L0_ffn_keys"]),
            a0idx=idx, Wq_sel=Wq_sel, bq1=bq_sel,
            Wo=np.asarray(inputs["L1_attn_Wo"]),
            Mm=np.ascontiguousarray(np.asarray(inputs["L1_attn_M"]).reshape(1024, 128)),
            zT8=np.asarray(inputs["L1_attn_z"]),
            qwgu=np.asarray(inputs["L1_ffn_qweight_gu"]),
            qzgu=np.asarray(inputs["L1_ffn_qzeros_gu"]),
            scgu=np.asarray(inputs["L1_ffn_scales_gu"]),
            qwdn=np.asarray(inputs["L1_ffn_qweight_dn"]),
            qzdn=np.asarray(inputs["L1_ffn_qzeros_dn"]),
            scdn=np.asarray(inputs["L1_ffn_scales_dn"]),
            sel=_SEL,
        ))
    return in_maps


def kernel_fused(inputs, T=8192):
    x = np.asarray(inputs["L0_attn_x"])
    nc = build_fused(T)
    in_maps = _fused_in_maps(inputs, T)
    res = run_bass_kernel_spmd(nc, in_maps, core_ids=list(range(8)))
    o = np.concatenate([res.results[r]["out"] for r in range(8)], 0)
    return o.reshape(x.shape[0], x.shape[1], D)


def kernel(**inputs):
    _ = inputs["L1_attn_Wk"], inputs["L1_attn_Wv"]
    return kernel_fused(inputs, T=np.asarray(inputs["L0_attn_x"]).shape[1])


def kernel_unfused(**inputs):
    x = np.asarray(inputs["L0_attn_x"])
    attn0 = run_attn0(x, np.asarray(inputs["L0_attn_bv_q"]),
                      np.asarray(inputs["L0_attn_bv_k"]),
                      np.asarray(inputs["L0_attn_bv_v"]))
    x1, xn1 = run_ffn0(x, attn0, np.asarray(inputs["L0_ffn_A_w"]),
                       np.asarray(inputs["L0_ffn_B_w"]),
                       np.asarray(inputs["L0_ffn_Wq_w"]),
                       np.asarray(inputs["L0_ffn_keys"]))
    attn1 = run_attn1(xn1, np.asarray(inputs["L1_attn_Wq"]),
                      np.asarray(inputs["L1_attn_bq"]),
                      np.asarray(inputs["L1_attn_Wo"]),
                      np.asarray(inputs["L1_attn_M"]),
                      np.asarray(inputs["L1_attn_z"]))
    out = run_ffn1(xn1, x1, attn1,
                   np.asarray(inputs["L1_ffn_qweight_gu"]),
                   np.asarray(inputs["L1_ffn_qzeros_gu"]),
                   np.asarray(inputs["L1_ffn_scales_gu"]),
                   np.asarray(inputs["L1_ffn_qweight_dn"]),
                   np.asarray(inputs["L1_ffn_qzeros_dn"]),
                   np.asarray(inputs["L1_ffn_scales_dn"]))
    return out


# version 14
# speedup vs baseline: 1.4447x; 1.4447x over previous
import numpy as np
import ml_dtypes
from contextlib import ExitStack

import concourse.bass as bass
import concourse.mybir as mybir
from concourse.bass_utils import run_bass_kernel_spmd

F32 = mybir.dt.float32
BF16 = mybir.dt.bfloat16
I32 = mybir.dt.int32
AF = mybir.ActivationFunctionType
ALU = mybir.AluOpType
AX = mybir.AxisListType
BF = ml_dtypes.bfloat16

D = 4096
HD0 = 1024



class Sem:
    def __init__(self, ctx, nc, name):
        self.h = ctx.enter_context(nc.semaphore(name))
        self.name = name
        self.n = 0


class Prog:

    ENGINES = ("sync", "vector", "scalar", "tensor", "gpsimd")

    def __init__(self):
        self.ops = {e: [] for e in self.ENGINES}
        self.lastwait = {e: {} for e in self.ENGINES}
        self.esem = {}

    def serialize(self, vector=None, scalar=None):
        if vector is not None:
            self.esem["vector"] = vector
        if scalar is not None:
            self.esem["scalar"] = scalar

    def op(self, eng, fn, waits=(), inc=None, inck=1):
        es = self.esem.get(eng)
        wl = []
        if es is not None and es.n > 0:
            wl.append((es, es.n))
            self.lastwait[eng][es.name] = es.n
        for s, v in waits:
            if v <= 0:
                continue
            lw = self.lastwait[eng].get(s.name, -1)
            if v > lw:
                wl.append((s, v))
                self.lastwait[eng][s.name] = v
        if es is not None:
            assert inc is None or inc is es
            inc, inck = es, 1
        self.ops[eng].append((wl, fn, inc, inck))
        if inc is not None:
            inc.n += inck
            return inc.n
        return None

    def barrier(self, SP, extra=()):
        waits = [(SP("s_v"), SP("s_v").n), (SP("s_s"), SP("s_s").n),
                 (SP("s_t"), SP("s_t").n)] + list(extra)
        for ename in self.ENGINES:
            self.op(ename, lambda e: e.nop(), waits=list(waits))

    def emit(self, block):
        for ename in self.ENGINES:
            oplist = self.ops[ename]

            def body(eng, oplist=oplist):
                for wl, fn, inc, inck in oplist:
                    for s, v in wl:
                        eng.wait_ge(s.h, v)
                    ins = fn(eng)
                    if inc is not None:
                        ins.then_inc(inc.h, inck)

            getattr(block, ename)(body)


class SemPool:
    def __init__(self, ctx, nc):
        self.ctx, self.nc, self.d = ctx, nc, {}

    def __call__(self, name):
        if name not in self.d:
            self.d[name] = Sem(self.ctx, self.nc, name)
        return self.d[name]


def new_prog(T=None, NT=None):
    nc = bass.Bass("TRN2", target_bir_lowering=False, debug=False)
    return nc, Prog()



def _pfx(nc, pre):
    def sb(name, shape, dt):
        return nc.sbuf_tensor(pre + name, shape, dt)

    def ps(name, shape, dt):
        return nc.psum_tensor(pre + name, shape, dt)

    return sb, ps



def emit_attn0(nc, p, SP, T, xb, bvq8, bvk8, bvv1, ident, identf, triu, attn0,
               vbuf):
    NK = T // 128
    NC = T // 256
    sb, ps = _pfx(nc, "A_")
    with ExitStack() as ctx:
        E = ctx.enter_context
        xnT = E(sb("xnT", [128, 8 * T], BF16))
        id_s = E(sb("id_s", [128, 128], BF16))
        idf_s = E(sb("idf_s", [128, 128], F32))
        triu_s = E(sb("triu_s", [128, 128], BF16))
        bq_s = E(sb("bq_s", [8, 128], F32))
        bk_s = E(sb("bk_s", [8, 128], F32))
        bv_s = E(sb("bv_s", [1, HD0], F32))
        sq_s = E(sb("sq_s", [8, 128], F32))
        sk_s = E(sb("sk_s", [8, 128], F32))
        g8_s = E(sb("g8_s", [8, 128], BF16))
        gcol = E(sb("gcol", [128, 8], F32))
        apart = E(sb("apart", [8, 2], F32))
        ones8 = E(sb("ones8", [8, 1], F32))
        ones1 = E(sb("ones1", [1, 128], F32))
        asum = E(sb("asum", [1, 2], F32))
        acomb = E(sb("acomb", [1, 1], F32))
        sc128 = E(sb("sc128", [128, 1], F32))
        eps_s = E(sb("eps_s", [128, 1], F32))
        vbind = E(sb("vbind", [1, HD0], F32))
        vb_s = E(sb("vb_s", [128, HD0], F32))
        scr2 = E(sb("scr2", [1, 2], F32))
        x_s = [E(sb(f"x_s{i}", [128, D], F32)) for i in range(2)]
        stat = [E(sb(f"stat{i}", [128, 48], F32)) for i in range(2)]
        aggr = [E(sb(f"aggr{i}", [128, 4], F32)) for i in range(2)]
        xn_s = [E(sb(f"xn_s{i}", [128, HD0], F32)) for i in range(2)]
        v_sb = [E(sb(f"v_sb{i}", [128, HD0], BF16)) for i in range(2)]
        qq_s = [E(sb("qq_s0", [128, 8 * 256], BF16))] * 2
        vk_s = [E(sb(f"vk_s{i}", [128, HD0], BF16)) for i in range(2)]
        at_s = [E(sb(f"at_s{i}", [128, 256], BF16)) for i in range(2)]
        oev = [E(sb("oev0", [128, 2 * HD0], F32))] * 2

        s_ld = SP("A_ld")
        s_x = [SP(f"A_x{i}") for i in range(2)]
        s_vs = [SP(f"A_vs{i}") for i in range(2)]
        s_vl = [SP(f"A_vl{i}") for i in range(2)]
        s_a0 = SP("A_a0")
        s_a1 = SP("A_a1")
        s_v = SP("s_v")
        s_s = SP("s_s")
        s_t = SP("s_t")
        p.serialize(vector=s_v, scalar=s_s)

        with ExitStack() as psA:
            p_a = psA.enter_context(ps("p_a", [1, 2], F32))
            p_b = psA.enter_context(ps("p_b", [128, 1], F32))
            p_vb = psA.enter_context(ps("p_vb", [128, HD0], F32))
            p_g = psA.enter_context(ps("p_g", [128, 8], BF16))
            p_tr = psA.enter_context(ps("p_tr", [128, 8 * 128], F32))

            p.op("sync", lambda e: e.dma_start(out=id_s[:], in_=ident), inc=s_ld, inck=16)
            p.op("sync", lambda e: e.dma_start(out=idf_s[:], in_=identf), inc=s_ld, inck=16)
            p.op("sync", lambda e: e.dma_start(out=triu_s[:], in_=triu), inc=s_ld, inck=16)
            p.op("sync", lambda e: e.dma_start(out=bq_s[:], in_=bvq8), inc=s_ld, inck=16)
            p.op("sync", lambda e: e.dma_start(out=bk_s[:], in_=bvk8), inc=s_ld, inck=16)
            ld0 = p.op("sync", lambda e: e.dma_start(out=bv_s[:], in_=bvv1),
                       inc=s_ld, inck=16)

            p.op("vector", lambda e: e.memset(ones8[:], 1.0), waits=[(s_ld, ld0)])
            p.op("vector", lambda e: e.memset(ones1[:], 1.0))
            v_eps = p.op("vector", lambda e: e.memset(eps_s[:], 1e-5), inc=s_v)
            p.op("vector", lambda e: e.tensor_reduce(
                apart[:, 0:1], bq_s[:], axis=AX.X, op=ALU.add, apply_absolute_value=True))
            va = p.op("vector", lambda e: e.tensor_reduce(
                apart[:, 1:2], bk_s[:], axis=AX.X, op=ALU.add, apply_absolute_value=True),
                inc=s_v)
            t_a = p.op("tensor", lambda e: e.matmul(p_a[:], ones8[:], apart[:],
                                                    start=True, stop=True),
                       waits=[(s_v, va)], inc=s_t)
            c_a = p.op("scalar", lambda e: e.copy(asum[:], p_a[:]),
                       waits=[(s_t, t_a)], inc=s_s)
            p.op("vector", lambda e: e.tensor_tensor(acomb[:], asum[:, 0:1],
                                                     asum[:, 1:2], op=ALU.mult),
                 waits=[(s_s, c_a)])
            v_ac = p.op("vector", lambda e: e.tensor_scalar(
                acomb[:], acomb[:], 0.125 / (HD0 * HD0), None, op0=ALU.mult), inc=s_v)
            t_b = p.op("tensor", lambda e: e.matmul(p_b[:], ones1[:], acomb[:],
                                                    start=True, stop=True),
                       waits=[(s_v, v_ac)], inc=s_t)
            c_b = p.op("scalar", lambda e: e.copy(sc128[:], p_b[:]),
                       waits=[(s_t, t_b)], inc=s_s)

            p.op("vector", lambda e: e.tensor_scalar(sq_s[:], bq_s[:], 0.0, None,
                                                     op0=ALU.is_lt))
            p.op("vector", lambda e: e.tensor_scalar(sq_s[:], sq_s[:], -2.0, 1.0,
                                                     op0=ALU.mult, op1=ALU.add))
            p.op("vector", lambda e: e.tensor_scalar(sk_s[:], bk_s[:], 0.0, None,
                                                     op0=ALU.is_lt))
            p.op("vector", lambda e: e.tensor_scalar(sk_s[:], sk_s[:], -2.0, 1.0,
                                                     op0=ALU.mult, op1=ALU.add))
            v_g8 = p.op("vector", lambda e: e.tensor_tensor(g8_s[:], sq_s[:], sk_s[:],
                                                            op=ALU.mult), inc=s_v)
            t_g = p.op("tensor", lambda e: e.transpose(p_g[:], g8_s[:], id_s[0:8, 0:8]),
                       waits=[(s_v, v_g8)], inc=s_t)
            c_g = p.op("scalar", lambda e: e.copy(gcol[:], p_g[:]),
                       waits=[(s_t, t_g)], inc=s_s)

            v_r = p.op("vector", lambda e: e.tensor_reduce(
                scr2[:, 0:1], bv_s[:], axis=AX.X, op=ALU.add, apply_absolute_value=True),
                inc=s_v)
            p.op("vector", lambda e: e.tensor_scalar(scr2[:, 0:1], scr2[:, 0:1],
                                                     1.0 / HD0, None, op0=ALU.mult),
                 waits=[(s_v, v_r)])
            p.op("vector", lambda e: e.tensor_scalar(vbind[:], bv_s[:], 0.0, None,
                                                     op0=ALU.is_lt))
            p.op("vector", lambda e: e.tensor_scalar(vbind[:], vbind[:], -2.0, 1.0,
                                                     op0=ALU.mult, op1=ALU.add))
            v_vb = p.op("vector", lambda e: e.tensor_scalar(
                vbind[:], vbind[:], scr2[:, 0:1], None, op0=ALU.mult), inc=s_v)
            p.op("tensor", lambda e: e.matmul(p_vb[:, 0:512], ones1[:], vbind[:, 0:512],
                                              start=True, stop=True),
                 waits=[(s_v, v_vb)])
            t_vb = p.op("tensor", lambda e: e.matmul(p_vb[:, 512:1024], ones1[:],
                                                     vbind[:, 512:1024],
                                                     start=True, stop=True), inc=s_t)
            c_vb = p.op("scalar", lambda e: e.copy(vb_s[:], p_vb[:]),
                        waits=[(s_t, t_vb)], inc=s_s)

            m_xld = {}
            m_nrm = {}
            m_tr = {}
            m_xc = {}
            hs = 0
            for k in range(NK):
                b = k % 2
                w = [] if k < 2 else [(s_v, m_nrm[k - 2])]
                m_xld[k] = p.op(
                    "sync",
                    lambda e, k=k, b=b: e.dma_start(out=x_s[b][:],
                                                    in_=xb[k * 128:(k + 1) * 128, :]),
                    waits=w, inc=s_x[b], inck=16)

                for i in range(8):
                    ins = p.op("vector",
                               lambda e, b=b, i=i: e.bn_stats(
                                   stat[b][:, 6 * i:6 * i + 6],
                                   x_s[b][:, 512 * i:512 * (i + 1)]),
                               waits=[(s_x[b], m_xld[k])],
                               inc=s_v if i == 7 else None)
                    if i == 7:
                        v_bns = ins
                v_bna = p.op("vector", lambda e, b=b: e.bn_aggr(aggr[b][:, 0:2],
                                                                stat[b][:]),
                             waits=[(s_v, v_bns)], inc=s_v)
                s_sd = p.op("scalar", lambda e, b=b: e.activation(
                    aggr[b][:, 2:3], aggr[b][:, 1:2], AF.Sqrt, bias=eps_s[:, 0:1],
                    scale=1.0),
                    waits=[(s_v, v_bna), (s_v, v_eps)], inc=s_s)
                v_rc = p.op("vector", lambda e, b=b: e.reciprocal(
                    aggr[b][:, 3:4], aggr[b][:, 2:3]), waits=[(s_s, s_sd)], inc=s_v)
                w = [(s_v, v_rc)]
                if k >= 2:
                    w.append((s_t, m_tr[k - 2]))
                    w.append((s_vs[b], 16 * ((k - 2) // 2 + 1)))
                m_nrm[k] = p.op("vector", lambda e, b=b: e.tensor_scalar(
                    xn_s[b][:], x_s[b][:, hs:hs + HD0], aggr[b][:, 0:1],
                    aggr[b][:, 3:4], op0=ALU.subtract, op1=ALU.mult),
                    waits=w, inc=s_v)
                v_v = p.op("vector", lambda e, b=b: e.tensor_tensor(
                    v_sb[b][:], xn_s[b][:], vb_s[:], op=ALU.mult),
                    waits=[(s_s, c_vb)], inc=s_v)
                p.op("gpsimd", lambda e, b=b, k=k: e.dma_start(
                    out=vbuf[k * 128:(k + 1) * 128, :], in_=v_sb[b][:]),
                    waits=[(s_v, v_v)], inc=s_vs[b], inck=16)
                w = [(s_v, m_nrm[k])]
                if k >= 1:
                    w.append((s_s, m_xc[k - 1]))
                for dt in range(8):
                    ins = p.op("tensor", lambda e, b=b, dt=dt: e.transpose(
                        p_tr[:, dt * 128:(dt + 1) * 128],
                        xn_s[b][:, dt * 128:(dt + 1) * 128], idf_s[:]),
                        waits=w if dt == 0 else (), inc=s_t if dt == 7 else None)
                m_tr[k] = ins
                dst = xnT[:].rearrange("p (dt t) -> p dt t", dt=8)[:, :, k * 128:(k + 1) * 128]
                m_xc[k] = p.op("scalar", lambda e, dst=dst: e.copy(
                    dst, p_tr[:].rearrange("p (dt t) -> p dt t", dt=8)),
                    waits=[(s_t, m_tr[k])], inc=s_s)

        with ExitStack() as psB:
            p_sc = [psB.enter_context(ps(f"p_sc{i}", [128, 256], F32))
                    for i in range(2)]
            p_out = [psB.enter_context(ps(f"p_out{i}", [128, HD0], F32))
                     for i in range(2)]

            m_qq = {}
            m_scmm = {}
            m_sig = {}
            m_diag = {}
            m_av = {}
            m_evs = {}
            m_evv = {}
            m_vld = {}
            xnT3 = xnT[:].rearrange("p (dt t) -> p dt t", dt=8)
            for qc in range(NC):
                w = [(s_s, m_xc[NK - 1]), (s_s, c_g)]
                if qc >= 1:
                    w.append((s_t, m_scmm[qc - 1]))
                for dt in range(8):
                    ins = p.op("vector", lambda e, dt=dt, qc=qc: e.tensor_scalar(
                        qq_s[qc % 2][:, dt * 256:(dt + 1) * 256],
                        xnT3[:, dt, qc * 256:(qc + 1) * 256],
                        gcol[:, dt:dt + 1], None, op0=ALU.mult),
                        waits=w if dt == 0 else (), inc=s_v if dt == 7 else None)
                m_qq[qc] = ins

                nk_q = 2 * qc + 2
                for k in range(nk_q):
                    vis = qc * qc + qc + k
                    ab = vis % 2
                    w = [(s_vs[0], 16 * ((NK + 1) // 2)), (s_vs[1], 16 * (NK // 2))]
                    if vis >= 2:
                        w.append((s_t, m_av[vis - 2]))
                    m_vld[vis] = p.op("sync", lambda e, k=k, ab=ab: e.dma_start(
                        out=vk_s[ab][:], in_=vbuf[k * 128:(k + 1) * 128, :]),
                        waits=w, inc=s_vl[ab], inck=16)
                    w = [(s_v, m_qq[qc])]
                    if vis >= 2:
                        w.append((s_s, m_sig[vis - 2]))
                    for dt in range(8):
                        ins = p.op("tensor", lambda e, dt=dt, k=k, qc=qc, ab=ab: e.matmul(
                            p_sc[ab][:],
                            xnT3[:, dt, k * 128:(k + 1) * 128],
                            qq_s[qc % 2][:, dt * 256:(dt + 1) * 256],
                            start=(dt == 0), stop=(dt == 7)),
                            waits=w if dt == 0 else (),
                            inc=s_t if dt == 7 else None)
                    m_scmm_vis = ins
                    if k == nk_q - 1:
                        m_scmm[qc] = ins
                    w = [(s_t, m_scmm_vis), (s_s, c_b)]
                    if vis >= 2:
                        w.append((s_t, m_av[vis - 2]))
                    m_sig[vis] = p.op("scalar", lambda e, ab=ab: e.activation(
                        at_s[ab][:], p_sc[ab][:], AF.Sigmoid, scale=sc128[:, 0:1]),
                        waits=w, inc=s_s)
                    diag = None
                    if k == 2 * qc:
                        diag = p.op("vector", lambda e, ab=ab: e.tensor_tensor(
                            at_s[ab][:, 0:128], at_s[ab][:, 0:128], triu_s[:],
                            op=ALU.mult), waits=[(s_s, m_sig[vis])], inc=s_v)
                    elif k == 2 * qc + 1:
                        p.op("vector", lambda e, ab=ab: e.memset(at_s[ab][:, 0:128], 0.0),
                             waits=[(s_s, m_sig[vis])])
                        diag = p.op("vector", lambda e, ab=ab: e.tensor_tensor(
                            at_s[ab][:, 128:256], at_s[ab][:, 128:256], triu_s[:],
                            op=ALU.mult), inc=s_v)
                    m_diag[vis] = diag
                    w = [(s_vl[vis % 2], m_vld[vis])]
                    if diag is not None:
                        w.append((s_v, diag))
                    else:
                        w.append((s_s, m_sig[vis]))
                    if k == 0 and qc >= 1:
                        w.append((s_s, m_evs[qc - 1]))
                        w.append((s_v, m_evv[qc - 1]))
                    for th in range(2):
                        for dh in range(2):
                            last = th == 1 and dh == 1
                            ins = p.op("tensor", lambda e, th=th, dh=dh, ab=ab, k=k, nk_q=nk_q: e.matmul(
                                p_out[th][:, dh * 512:(dh + 1) * 512],
                                at_s[ab][:, th * 128:(th + 1) * 128],
                                vk_s[ab][:, dh * 512:(dh + 1) * 512],
                                start=(k == 0), stop=(k == nk_q - 1)),
                                waits=w if (th == 0 and dh == 0) else (),
                                inc=s_t if last else None)
                    m_av[vis] = ins

                w = [(s_t, m_av[qc * qc + qc + nk_q - 1])]
                if qc >= 1:
                    w.append((s_a0, 16 * qc))
                    w.append((s_a1, 16 * qc))
                m_evs[qc] = p.op("scalar", lambda e, qc=qc: e.copy(
                    oev[qc % 2][:, 0:HD0], p_out[0][:]), waits=w, inc=s_s)
                m_evv[qc] = p.op("vector", lambda e, qc=qc: e.tensor_copy(
                    oev[qc % 2][:, HD0:2 * HD0], p_out[1][:]), waits=w, inc=s_v)
                p.op("gpsimd", lambda e, qc=qc: e.dma_start(
                    out=attn0[qc * 256:qc * 256 + 128, :], in_=oev[qc % 2][:, 0:HD0]),
                    waits=[(s_s, m_evs[qc])], inc=s_a0, inck=16)
                p.op("gpsimd", lambda e, qc=qc: e.dma_start(
                    out=attn0[qc * 256 + 128:qc * 256 + 256, :],
                    in_=oev[qc % 2][:, HD0:2 * HD0]),
                    waits=[(s_v, m_evv[qc])], inc=s_a1, inck=16)

            p.op("gpsimd", lambda e: e.nop(),
                 waits=[(s_a0, 16 * NC), (s_a1, 16 * NC)])
    return NC


def build_attn0(T):
    nc = bass.Bass("TRN2", target_bir_lowering=False, debug=False)
    xb = nc.dram_tensor("xb", [T, D], F32, kind="ExternalInput").ap()
    bvq8 = nc.dram_tensor("bvq8", [8, 128], F32, kind="ExternalInput").ap()
    bvk8 = nc.dram_tensor("bvk8", [8, 128], F32, kind="ExternalInput").ap()
    bvv1 = nc.dram_tensor("bvv1", [1, HD0], F32, kind="ExternalInput").ap()
    ident = nc.dram_tensor("ident", [128, 128], BF16, kind="ExternalInput").ap()
    identf = nc.dram_tensor("identf", [128, 128], F32, kind="ExternalInput").ap()
    triu = nc.dram_tensor("triu", [128, 128], BF16, kind="ExternalInput").ap()
    attn0 = nc.dram_tensor("attn0", [T, HD0], F32, kind="ExternalOutput").ap()
    vbuf = nc.dram_tensor("vbuf", [T, HD0], BF16, kind="Internal").ap()
    p = Prog()
    with ExitStack() as ctx:
        SP = SemPool(ctx, nc)
        emit_attn0(nc, p, SP, T, xb, bvq8, bvk8, bvv1, ident, identf, triu,
                   attn0, vbuf)
        block = ctx.enter_context(nc.Block())
        p.emit(block)
    return nc



_IDENT = np.eye(128, dtype=BF)
_IDENTF = np.eye(128, dtype=np.float32)
_TRIU = np.triu(np.ones((128, 128), dtype=BF))


def run_attn0(x, bv_q, bv_k, bv_v, T=None):
    B, Tx, _ = x.shape
    T = T or Tx
    nc = build_attn0(T)
    in_maps = []
    for r in range(8):
        b, h = r // 4, r % 4
        xr = np.ascontiguousarray(np.roll(x[b], -HD0 * h, axis=1))
        in_maps.append(dict(
            xb=xr,
            bvq8=np.ascontiguousarray(bv_q[h].reshape(8, 128)),
            bvk8=np.ascontiguousarray(bv_k[h].reshape(8, 128)),
            bvv1=np.ascontiguousarray(bv_v[h].reshape(1, HD0)),
            ident=_IDENT, identf=_IDENTF, triu=_TRIU,
        ))
    res = run_bass_kernel_spmd(nc, in_maps, core_ids=list(range(8)))
    out = np.empty((B, T, D), np.float32)
    for r in range(8):
        b, h = r // 4, r % 4
        out[b, :, HD0 * h:HD0 * (h + 1)] = res.results[r]["attn0"]
    return out




def emit_ffn0(nc, p, SP, NT, xo, a0, A_w, B_w, Wq_w, keys, identf, x1, xn1,
              a0g=None, a0idx=None, pre_waits=()):
    Ntok = NT * 128
    sb, ps = _pfx(nc, "B_")
    with ExitStack() as ctx:
        E = ctx.enter_context
        idf_s = E(sb("idf_s", [128, 128], F32))
        id_bf = E(sb("id_bf", [128, 128], BF16))
        eps_s = E(sb("eps_s", [128, 1], F32))
        A_T = E(sb("A_T", [128, 32 * 512], BF16))
        B_T = E(sb("B_T", [128, 4 * D], BF16))
        WqT = E(sb("WqT", [128, 32 * 16], BF16))
        keysT = E(sb("keysT", [16, 64], BF16))
        wq_s = E(sb("wq_s", [16, D], F32))
        wq_bf = E(sb("wq_bf", [16, D], BF16))
        key_s = E(sb("key_s", [64, 16], F32))
        key_bf = E(sb("key_bf", [64, 16], BF16))
        x_s = [E(sb(f"x_s{i}", [128, D], F32)) for i in range(2)]
        a_s = [E(sb("a_s0", [128, D], F32))] * 2
        stat = E(sb("stat", [128, 48], F32))
        aggr = E(sb("aggr", [128, 4], F32))
        xn_f = E(sb("xn_f", [128, D], F32))
        xnT_t = E(sb("xnT_t", [128, D], BF16))
        qT_bf = E(sb("qT_bf", [16, 128], BF16))
        sco_s = E(sb("sco_s", [128, 64], F32))
        zap_s = E(sb("zap_s", [128, 64], F32))
        m16 = E(sb("m16", [128, 16], F32))
        nm1 = E(sb("nm1", [128, 1], F32))
        gmask = E(sb("gmask", [128, 64], F32))
        gexp = E(sb("gexp", [128, 64], F32))
        gden = E(sb("gden", [128, 2], F32))
        gate = E(sb("gate", [128, 64], F32))
        zg_bf = E(sb("zg_bf", [128, 512], BF16))
        zgT = E(sb("zgT", [128, 512], BF16))
        x1_s = [E(sb(f"x1_s{i}", [128, D], F32)) for i in range(2)]
        xn1_s = [E(sb("xn1_s0", [128, D], BF16))] * 2

        s_ld = SP("B_ld")
        s_wld = SP("B_wld")
        s_x = [SP(f"B_x{i}") for i in range(2)]
        s_a = SP("B_a")
        s_st1 = [SP(f"B_st1{i}") for i in range(2)]
        s_st2 = SP("B_st2")
        s_v = SP("s_v")
        s_s = SP("s_s")
        s_t = SP("s_t")
        p.serialize(vector=s_v, scalar=s_s)
        if a0g is not None:
            idx_s = E(sb("a0idx_s", [128, 4 * NT], I32))

        p_tr = E(ps("p_tr", [128, 8 * 128], F32))
        p_trb = E(ps("p_trb", [128, 8 * 128], BF16))
        p_q = E(ps("p_q", [16, 128], F32))
        p_sco = E(ps("p_sco", [128, 64], F32))
        p_z = E(ps("p_z", [128, 512], F32))
        p_f = E(ps("p_f", [128, 512], F32))

        p.op("sync", lambda e: e.dma_start(out=idf_s[:], in_=identf),
             waits=list(pre_waits), inc=s_ld, inck=16)
        p.op("sync", lambda e: e.dma_start(out=wq_s[:], in_=Wq_w), inc=s_ld, inck=16)
        p.op("sync", lambda e: e.dma_start(out=key_s[:], in_=keys), inc=s_ld, inck=16)
        if a0g is not None:
            p.op("sync", lambda e: e.dma_start(out=idx_s[:], in_=a0idx),
                 inc=s_ld, inck=16)
        ld_all = s_ld.n
        v_idb = p.op("vector", lambda e: e.tensor_copy(id_bf[:], idf_s[:]),
                     waits=[(s_ld, ld_all)], inc=s_v)
        p.op("vector", lambda e: e.memset(eps_s[:], 1e-5))
        v_wq = p.op("vector", lambda e: e.tensor_copy(wq_bf[:], wq_s[:]), inc=s_v)
        v_k = p.op("vector", lambda e: e.tensor_copy(key_bf[:], key_s[:]), inc=s_v)
        for dt in range(16):
            ins = p.op("tensor", lambda e, dt=dt: e.transpose(
                p_trb[:, dt * 16:dt * 16 + 16],
                wq_bf[:, dt * 128:(dt + 1) * 128], id_bf[0:16, 0:16]),
                waits=[(s_v, v_wq), (s_v, v_idb)] if dt == 0 else (),
                inc=s_t if dt == 15 else None)
        t_wq1 = ins
        c_wq1 = p.op("scalar", lambda e: e.copy(WqT[:, 0:256], p_trb[:, 0:256]),
                     waits=[(s_t, t_wq1)], inc=s_s)
        for dt in range(16, 32):
            ins = p.op("tensor", lambda e, dt=dt: e.transpose(
                p_trb[:, (dt - 16) * 16:(dt - 16) * 16 + 16],
                wq_bf[:, dt * 128:(dt + 1) * 128], id_bf[0:16, 0:16]),
                waits=[(s_s, c_wq1)] if dt == 16 else (),
                inc=s_t if dt == 31 else None)
        t_wq2 = ins
        c_wq2 = p.op("scalar", lambda e: e.copy(WqT[:, 256:512], p_trb[:, 0:256]),
                     waits=[(s_t, t_wq2)], inc=s_s)
        t_k = p.op("tensor", lambda e: e.transpose(p_trb[0:16, 0:64], key_bf[:],
                                                   id_bf[0:64, 0:64]),
                   waits=[(s_v, v_k), (s_s, c_wq2)], inc=s_t)
        c_k = p.op("scalar", lambda e: e.copy(keysT[:], p_trb[0:16, 0:64]),
                   waits=[(s_t, t_k)], inc=s_s)

        wld = x_s[0]
        wld_bf = xnT_t
        prev_cp = c_k
        for i in range(4):
            ldw = p.op("sync", lambda e, i=i: e.dma_start(
                out=wld[:], in_=A_w[i * 128:(i + 1) * 128, :]),
                waits=[(s_s, prev_cp)], inc=s_wld, inck=16)
            vcp = p.op("vector", lambda e: e.tensor_copy(wld_bf[:], wld[:]),
                       waits=[(s_wld, ldw)], inc=s_v)
            for dt in range(32):
                ins = p.op("tensor", lambda e, dt=dt: e.transpose(
                    p_trb[:, (dt % 8) * 128:(dt % 8 + 1) * 128],
                    wld_bf[:, dt * 128:(dt + 1) * 128], id_bf[:]),
                    waits=([(s_v, vcp)] if dt == 0 else
                           ([(s_s, s_s.n)] if dt % 8 == 0 else ())),
                    inc=s_t if dt % 8 == 7 else None)
                if dt % 8 == 7:
                    grp = dt // 8
                    dst = A_T[:].rearrange("p (dt c) -> p dt c", c=512)[
                        :, grp * 8:(grp + 1) * 8, i * 128:(i + 1) * 128]
                    prev_cp = p.op("scalar", lambda e, dst=dst: e.copy(
                        dst, p_trb[:].rearrange("p (dt c) -> p dt c", c=128)),
                        waits=[(s_t, ins)], inc=s_s)
        for i in range(32):
            ldw = p.op("sync", lambda e, i=i: e.dma_start(
                out=wld[:, 0:512], in_=B_w[i * 128:(i + 1) * 128, :]),
                waits=[(s_s, prev_cp)], inc=s_wld, inck=16)
            vcp = p.op("vector", lambda e: e.tensor_copy(wld_bf[:, 0:512],
                                                         wld[:, 0:512]),
                       waits=[(s_wld, ldw)], inc=s_v)
            for nb in range(4):
                ins = p.op("tensor", lambda e, nb=nb: e.transpose(
                    p_trb[:, nb * 128:(nb + 1) * 128],
                    wld_bf[:, nb * 128:(nb + 1) * 128], id_bf[:]),
                    waits=[(s_v, vcp)] if nb == 0 else (),
                    inc=s_t if nb == 3 else None)
            dst = B_T[:].rearrange("p (nb c) -> p nb c", c=D)[
                :, :, i * 128:(i + 1) * 128]
            prev_cp = p.op("scalar", lambda e, dst=dst: e.copy(
                dst, p_trb[:, 0:512].rearrange("p (nb c) -> p nb c", c=128)),
                waits=[(s_t, ins)], inc=s_s)
        prep_done = prev_cp

        m_nrm = {}
        m_sq = {}
        m_trl = {}
        m_zg = {}
        m_xa = {}
        for t in range(NT):
            b = t % 2
            if t < 2:
                w = [(s_s, prep_done)]
            else:
                w = [(s_v, m_xa[t - 2])]
            ldx = p.op("sync", lambda e, t=t, b=b: e.dma_start(
                out=x_s[b][:], in_=xo[t * 128:(t + 1) * 128, :]), waits=w,
                inc=s_x[b], inck=16)
            w = [(s_ld, ld_all)] if t < 1 else [(s_v, m_xa[t - 1])]
            if a0g is None:
                lda = p.op("sync", lambda e, t=t, b=b: e.dma_start(
                    out=a_s[b][:], in_=a0[t * 128:(t + 1) * 128, :]),
                    waits=w, inc=s_a, inck=16)
            else:
                for h in range(4):
                    lda = p.op("gpsimd", lambda e, t=t, b=b, h=h: e.indirect_dma_start(
                        a_s[b][:, h * 1024:(h + 1) * 1024], None,
                        a0g, bass.IndirectOffsetOnAxis(
                            ap=idx_s[:, t * 4 + h:t * 4 + h + 1], axis=0)),
                        waits=w if h == 0 else (), inc=s_a, inck=16)
            for i in range(8):
                ins = p.op("vector", lambda e, b=b, i=i: e.bn_stats(
                    stat[:, 6 * i:6 * i + 6], x_s[b][:, 512 * i:512 * (i + 1)]),
                    waits=[(s_x[b], ldx)], inc=s_v if i == 7 else None)
            v_bns = ins
            v_bna = p.op("vector", lambda e: e.bn_aggr(aggr[:, 0:2], stat[:]),
                         waits=[(s_v, v_bns)], inc=s_v)
            s_sd = p.op("scalar", lambda e: e.activation(
                aggr[:, 2:3], aggr[:, 1:2], AF.Sqrt, bias=eps_s[:, 0:1], scale=1.0),
                waits=[(s_v, v_bna)], inc=s_s)
            v_rc = p.op("vector", lambda e: e.reciprocal(aggr[:, 3:4], aggr[:, 2:3]),
                        waits=[(s_s, s_sd)], inc=s_v)
            w = [(s_v, v_rc)]
            if t >= 1:
                w.append((s_t, m_trl[t - 1]))
            m_nrm[t] = p.op("vector", lambda e, b=b: e.tensor_scalar(
                xn_f[:], x_s[b][:], aggr[:, 0:1], aggr[:, 3:4],
                op0=ALU.subtract, op1=ALU.mult), waits=w, inc=s_v)
            for rnd in range(4):
                for j in range(8):
                    dt = rnd * 8 + j
                    wts = []
                    if rnd == 0 and j == 0:
                        wts = [(s_v, m_nrm[t]), (s_s, prep_done if t == 0 else m_sq[t - 1])]
                    elif j == 0:
                        wts = [(s_s, s_s.n)]
                    ins = p.op("tensor", lambda e, dt=dt, j=j: e.transpose(
                        p_tr[:, j * 128:(j + 1) * 128],
                        xn_f[:, dt * 128:(dt + 1) * 128], idf_s[:]),
                        waits=wts, inc=s_t if j == 7 else None)
                dst = xnT_t[:].rearrange("p (dt c) -> p dt c", c=128)[
                    :, rnd * 8:(rnd + 1) * 8, :]
                cp = p.op("scalar", lambda e, dst=dst: e.copy(
                    dst, p_tr[:].rearrange("p (dt c) -> p dt c", c=128)),
                    waits=[(s_t, ins)], inc=s_s)
            m_xt = cp
            m_trl[t] = s_t.n
            for dt in range(32):
                ins = p.op("tensor", lambda e, dt=dt: e.matmul(
                    p_q[:], WqT[:, dt * 16:(dt + 1) * 16],
                    xnT_t[:, dt * 128:(dt + 1) * 128],
                    start=(dt == 0), stop=(dt == 31)),
                    waits=[(s_s, m_xt)] if dt == 0 else (),
                    inc=s_t if dt == 31 else None)
            t_q = ins
            c_q = p.op("scalar", lambda e: e.copy(qT_bf[:], p_q[:]),
                       waits=[(s_t, t_q)], inc=s_s)
            t_sco = p.op("tensor", lambda e: e.matmul(
                p_sco[:], qT_bf[:], keysT[:], start=True, stop=True),
                waits=[(s_s, c_q)], inc=s_t)
            c_sco = p.op("scalar", lambda e: e.copy(sco_s[:], p_sco[:]),
                         waits=[(s_t, t_sco)], inc=s_s)
            v_m1 = p.op("vector", lambda e: e.max(out=m16[:, 0:8], in_=sco_s[:]),
                        waits=[(s_s, c_sco)], inc=s_v)
            v_zap = p.op("vector", lambda e: e.match_replace(
                out=zap_s[:], in_to_replace=m16[:, 0:8], in_values=sco_s[:],
                imm_value=-1e30), waits=[(s_v, v_m1)], inc=s_v)
            v_m2 = p.op("vector", lambda e: e.max(out=m16[:, 8:16], in_=zap_s[:]),
                        waits=[(s_v, v_zap)], inc=s_v)
            v_nm = p.op("vector", lambda e: e.tensor_scalar(
                nm1[:], m16[:, 0:1], -1.0, None, op0=ALU.mult),
                waits=[(s_v, v_m2)], inc=s_v)
            v_msk = p.op("vector", lambda e: e.tensor_scalar(
                gmask[:], sco_s[:], m16[:, 15:16], None, op0=ALU.is_ge),
                inc=s_v)
            s_exp = p.op("scalar", lambda e: e.activation(
                gexp[:], sco_s[:], AF.Exp, bias=nm1[:, 0:1], scale=1.0),
                waits=[(s_v, v_nm)], inc=s_s)
            v_gn = p.op("vector", lambda e: e.tensor_tensor(
                gexp[:], gexp[:], gmask[:], op=ALU.mult),
                waits=[(s_s, s_exp), (s_v, v_msk)], inc=s_v)
            v_dn = p.op("vector", lambda e: e.tensor_reduce(
                gden[:, 0:1], gexp[:], axis=AX.X, op=ALU.add),
                waits=[(s_v, v_gn)], inc=s_v)
            v_rc2 = p.op("vector", lambda e: e.reciprocal(gden[:, 1:2], gden[:, 0:1]),
                         waits=[(s_v, v_dn)], inc=s_v)
            v_gt = p.op("vector", lambda e: e.tensor_scalar(
                gate[:], gexp[:], gden[:, 1:2], 2.0, op0=ALU.mult, op1=ALU.mult),
                waits=[(s_v, v_rc2)], inc=s_v)
            for dt in range(32):
                w = []
                if dt == 0 and t >= 1:
                    w = [(s_v, m_zg[t - 1])]
                ins = p.op("tensor", lambda e, dt=dt: e.matmul(
                    p_z[:], xnT_t[:, dt * 128:(dt + 1) * 128],
                    A_T[:, dt * 512:(dt + 1) * 512],
                    start=(dt == 0), stop=(dt == 31)),
                    waits=w, inc=s_t if dt == 31 else None)
            t_z = ins
            gx = gate[:].rearrange("p (o u) -> p o u", u=1).to_broadcast([128, 64, 8])
            v_zg = p.op("vector", lambda e, gx=gx: e.tensor_tensor(
                zg_bf[:].rearrange("p (o u) -> p o u", u=8),
                p_z[:].rearrange("p (o u) -> p o u", u=8), gx, op=ALU.mult),
                waits=[(s_v, v_gt), (s_t, t_z)], inc=s_v)
            m_zg[t] = v_zg
            for nb in range(4):
                ins = p.op("tensor", lambda e, nb=nb: e.transpose(
                    p_trb[:, nb * 128:(nb + 1) * 128],
                    zg_bf[:, nb * 128:(nb + 1) * 128], id_bf[:]),
                    waits=[(s_v, v_zg)] if nb == 0 else (),
                    inc=s_t if nb == 3 else None)
            t_zgt = ins
            c_zgt = p.op("scalar", lambda e: e.copy(zgT[:], p_trb[:, 0:512]),
                         waits=[(s_t, t_zgt)], inc=s_s)
            w = [(s_a, lda)]
            if t >= 2:
                w.append((s_st1[b], 16 * ((t - 2) // 2 + 1)))
            v_xa = p.op("vector", lambda e, b=b: e.tensor_add(
                x1_s[b][:], x_s[b][:], a_s[b][:]), waits=w, inc=s_v)
            m_xa[t] = v_xa
            for oc in range(8):
                for nb in range(4):
                    w = []
                    if nb == 0:
                        w = [(s_v, s_v.n)]
                        if oc == 0:
                            w.append((s_s, c_zgt))
                    ins = p.op("tensor", lambda e, nb=nb, oc=oc: e.matmul(
                        p_f[:], zgT[:, nb * 128:(nb + 1) * 128],
                        B_T[:, nb * D + oc * 512:nb * D + (oc + 1) * 512],
                        start=(nb == 0), stop=(nb == 3)),
                        waits=w, inc=s_t if nb == 3 else None)
                t_f = ins
                v_acc = p.op("vector", lambda e, b=b, oc=oc: e.tensor_add(
                    x1_s[b][:, oc * 512:(oc + 1) * 512],
                    x1_s[b][:, oc * 512:(oc + 1) * 512], p_f[:]),
                    waits=[(s_t, t_f), (s_v, v_xa)], inc=s_v)
            for i in range(8):
                ins = p.op("vector", lambda e, b=b, i=i: e.bn_stats(
                    stat[:, 6 * i:6 * i + 6], x1_s[b][:, 512 * i:512 * (i + 1)]),
                    inc=s_v if i == 7 else None)
            v_bns2 = ins
            v_bna2 = p.op("vector", lambda e: e.bn_aggr(aggr[:, 0:2], stat[:]),
                          waits=[(s_v, v_bns2)], inc=s_v)
            s_sd2 = p.op("scalar", lambda e: e.activation(
                aggr[:, 2:3], aggr[:, 1:2], AF.Sqrt, bias=eps_s[:, 0:1], scale=1.0),
                waits=[(s_v, v_bna2)], inc=s_s)
            m_sq[t] = s_sd2
            v_rc3 = p.op("vector", lambda e: e.reciprocal(aggr[:, 3:4], aggr[:, 2:3]),
                         waits=[(s_s, s_sd2)], inc=s_v)
            w = [(s_v, v_rc3)]
            if t >= 1:
                w.append((s_st2, 16 * t))
            v_xn1 = p.op("vector", lambda e, b=b: e.tensor_scalar(
                xn1_s[b][:], x1_s[b][:], aggr[:, 0:1], aggr[:, 3:4],
                op0=ALU.subtract, op1=ALU.mult), waits=w, inc=s_v)
            p.op("gpsimd", lambda e, t=t, b=b: e.dma_start(
                out=x1[t * 128:(t + 1) * 128, :], in_=x1_s[b][:]),
                waits=[(s_v, v_xn1)], inc=s_st1[b], inck=16)
            p.op("gpsimd", lambda e, t=t, b=b: e.dma_start(
                out=xn1[t * 128:(t + 1) * 128, :], in_=xn1_s[b][:]),
                inc=s_st2, inck=16)

        p.op("gpsimd", lambda e: e.nop(),
             waits=[(s_st1[0], 16 * ((NT + 1) // 2)), (s_st1[1], 16 * (NT // 2)),
                    (s_st2, 16 * NT)])


def build_ffn0(NT=16):
    nc = bass.Bass("TRN2", target_bir_lowering=False, debug=False)
    Ntok = NT * 128
    xo = nc.dram_tensor("xo", [Ntok, D], F32, kind="ExternalInput").ap()
    a0 = nc.dram_tensor("a0", [Ntok, D], F32, kind="ExternalInput").ap()
    A_w = nc.dram_tensor("A_w", [512, D], F32, kind="ExternalInput").ap()
    B_w = nc.dram_tensor("B_w", [D, 512], F32, kind="ExternalInput").ap()
    Wq_w = nc.dram_tensor("Wq_w", [16, D], F32, kind="ExternalInput").ap()
    keys = nc.dram_tensor("keys", [64, 16], F32, kind="ExternalInput").ap()
    identf = nc.dram_tensor("identf", [128, 128], F32, kind="ExternalInput").ap()
    x1 = nc.dram_tensor("x1", [Ntok, D], F32, kind="ExternalOutput").ap()
    xn1 = nc.dram_tensor("xn1", [Ntok, D], BF16, kind="ExternalOutput").ap()
    p = Prog()
    with ExitStack() as ctx:
        SP = SemPool(ctx, nc)
        emit_ffn0(nc, p, SP, NT, xo, a0, A_w, B_w, Wq_w, keys, identf, x1, xn1)
        block = ctx.enter_context(nc.Block())
        p.emit(block)
    return nc


def run_ffn0(x, attn0, A_w, B_w, Wq_w, keys):
    B, T, _ = x.shape
    ntok = B * T // 8
    nc = build_ffn0(ntok // 128)
    xs = x.reshape(8, ntok, D)
    as_ = attn0.reshape(8, ntok, D)
    in_maps = []
    for r in range(8):
        in_maps.append(dict(
            xo=np.ascontiguousarray(xs[r]), a0=np.ascontiguousarray(as_[r]),
            A_w=A_w, B_w=B_w, Wq_w=Wq_w, keys=keys, identf=_IDENTF))
    res = run_bass_kernel_spmd(nc, in_maps, core_ids=list(range(8)))
    x1 = np.concatenate([res.results[r]["x1"] for r in range(8)], 0).reshape(B, T, D)
    xn1 = np.concatenate([res.results[r]["xn1"] for r in range(8)], 0).reshape(B, T, D)
    return x1, xn1




def emit_attn1(nc, p, SP, NT, xn1, Wq_sel, bq1, Wo, Mm, zT8, ident, attn1,
               pre_waits=()):
    Ntok = NT * 128
    sb, ps = _pfx(nc, "C_")
    with ExitStack() as ctx:
        E = ctx.enter_context
        id_bf = E(sb("id_bf", [128, 128], BF16))
        WqselT = E(sb("WqselT", [128, 32 * 1024], BF16))
        WoF = E(sb("WoF", [128, 8 * D], BF16))
        M_bf = E(sb("M_bf", [128, 8 * 128], BF16))
        zT_bf = E(sb("zT_bf", [128, 8], BF16))
        bq_b = E(sb("bq_b", [128, 1024], F32))
        bq_s = E(sb("bq_s", [1, 1024], F32))
        ones1 = E(sb("ones1", [1, 128], F32))
        z_s = E(sb("z_s", [8, 128], F32))
        z_bf = E(sb("z_bf", [8, 128], BF16))
        xt_bf = E(sb("xt_bf", [128, D], BF16))
        x1T_t = E(sb("x1T_t", [128, D], BF16))
        qb_f = E(sb("qb_f", [128, 1024], F32))
        qm_f = E(sb("qm_f", [128, 1024], F32))
        e1_f = E(sb("e1_f", [128, 1024], F32))
        sig_bf = E(sb("sig_bf", [128, 1024], BF16))
        sigT = E(sb("sigT", [128, 8 * 128], BF16))
        den_f = E(sb("den_f", [128, 8], F32))
        rcp8 = E(sb("rcp8", [128, 8], F32))
        retr = E(sb("retr", [128, 8 * 128], BF16))
        retrT = E(sb("retrT", [128, 8 * 128], BF16))
        ao_s = [E(sb("ao_s0", [128, D], F32))] * 2
        wst = ao_s[0]
        wst_bf = xt_bf

        s_set = SP("C_set")
        s_wst = SP("C_wst")
        s_x = SP("C_x")
        s_st = SP("C_st")
        s_v = SP("s_v")
        s_s = SP("s_s")
        s_t = SP("s_t")
        p.serialize(vector=s_v, scalar=s_s)

        p_trb = E(ps("p_trb", [128, 8 * 128], BF16))
        p_acc = E(ps("p_acc", [128, 8 * 128], F32))
        p_q = E(ps("p_q", [128, 512], F32))
        p_n = E(ps("p_n", [128, 8 * 128], F32))
        p_d = E(ps("p_d", [128, 8], F32))
        p_o = E(ps("p_o", [128, 512], F32))

        p.op("sync", lambda e: e.dma_start(out=id_bf[:], in_=ident),
             waits=list(pre_waits), inc=s_set, inck=16)
        p.op("sync", lambda e: e.dma_start(out=bq_s[:], in_=bq1),
             inc=s_set, inck=16)
        p.op("sync", lambda e: e.dma_start(out=z_s[:], in_=zT8),
             inc=s_set, inck=16)
        ld_all = s_set.n
        v_o1 = p.op("vector", lambda e: e.memset(ones1[:], 1.0), inc=s_v)
        v_zb = p.op("vector", lambda e: e.tensor_copy(z_bf[:], z_s[:]),
                    waits=[(s_set, ld_all)], inc=s_v)
        p.op("tensor", lambda e: e.matmul(p_acc[:, 0:512], ones1[:], bq_s[:, 0:512],
                                          start=True, stop=True),
             waits=[(s_v, v_o1), (s_set, ld_all)])
        t_bq = p.op("tensor", lambda e: e.matmul(p_acc[:, 512:1024], ones1[:],
                                                 bq_s[:, 512:1024], start=True,
                                                 stop=True), inc=s_t)
        c_bq = p.op("scalar", lambda e: e.copy(bq_b[:], p_acc[:]),
                    waits=[(s_t, t_bq)], inc=s_s)
        t_z = p.op("tensor", lambda e: e.transpose(p_trb[:, 0:8], z_bf[:],
                                                   id_bf[0:8, 0:8]),
                   waits=[(s_v, v_zb)], inc=s_t)
        c_z = p.op("scalar", lambda e: e.copy(zT_bf[:], p_trb[:, 0:8]),
                   waits=[(s_t, t_z)], inc=s_s)
        ld_m = p.op("sync", lambda e: e.dma_start(
            out=wst[:, 0:1024].rearrange("p (h e) -> p h e", e=128),
            in_=Mm.rearrange("(h p) e -> p h e", p=128)),
            inc=s_wst, inck=16)
        v_m = p.op("vector", lambda e: e.tensor_copy(M_bf[:], wst[:, 0:1024]),
                   waits=[(s_wst, ld_m)], inc=s_v)
        prev_cp = c_z
        for i in range(8):
            ldw = p.op("sync", lambda e, i=i: e.dma_start(
                out=wst[:], in_=Wq_sel[i * 128:(i + 1) * 128, :]),
                waits=[(s_s, prev_cp), (s_v, v_m)], inc=s_wst, inck=16)
            vcp = p.op("vector", lambda e: e.tensor_copy(wst_bf[:], wst[:]),
                       waits=[(s_wst, ldw)], inc=s_v)
            for grp in range(4):
                for j in range(8):
                    dt = grp * 8 + j
                    wts = [(s_v, vcp), (s_s, s_s.n)] if dt == 0 else (
                        [(s_s, s_s.n)] if j == 0 else ())
                    ins = p.op("tensor", lambda e, dt=dt, j=j: e.transpose(
                        p_trb[:, j * 128:(j + 1) * 128],
                        wst_bf[:, dt * 128:(dt + 1) * 128], id_bf[:]),
                        waits=wts, inc=s_t if j == 7 else None)
                dst = WqselT[:].rearrange("p (dt c) -> p dt c", c=1024)[
                    :, grp * 8:(grp + 1) * 8, i * 128:(i + 1) * 128]
                prev_cp = p.op("scalar", lambda e, dst=dst: e.copy(
                    dst, p_trb[:].rearrange("p (j c) -> p j c", c=128)),
                    waits=[(s_t, ins)], inc=s_s)
        for ot in range(32):
            ldw = p.op("sync", lambda e, ot=ot: e.dma_start(
                out=wst[:], in_=Wo[ot * 128:(ot + 1) * 128, :]),
                waits=[(s_s, prev_cp)], inc=s_wst, inck=16)
            vcp = p.op("vector", lambda e: e.tensor_copy(wst_bf[:], wst[:]),
                       waits=[(s_wst, ldw)], inc=s_v)
            for h in range(8):
                for g in range(4):
                    ins = p.op("tensor", lambda e, h=h, g=g: e.matmul(
                        p_acc[:, h * 128:(h + 1) * 128],
                        wst_bf[:, (4 * h + g) * 128:(4 * h + g + 1) * 128],
                        id_bf[:], start=(g == 0), stop=(g == 3)),
                        waits=[(s_v, vcp), (s_s, prev_cp)] if h == 0 and g == 0 else (),
                        inc=s_t if h == 7 and g == 3 else None)
            dst = WoF[:].rearrange("p (h c) -> p h c", c=D)[
                :, :, ot * 128:(ot + 1) * 128]
            prev_cp = p.op("scalar", lambda e, dst=dst: e.copy(
                dst, p_acc[:].rearrange("p (h c) -> p h c", c=128)),
                waits=[(s_t, ins)], inc=s_s)
        prep_done = prev_cp

        m_trl = {}
        for t in range(NT):
            w = [(s_s, prep_done)] if t == 0 else [(s_t, m_trl[t - 1])]
            ldx = p.op("sync", lambda e, t=t: e.dma_start(
                out=xt_bf[:], in_=xn1[t * 128:(t + 1) * 128, :]),
                waits=w, inc=s_x, inck=16)
            for grp in range(4):
                for j in range(8):
                    dt = grp * 8 + j
                    wts = [(s_x, ldx), (s_s, s_s.n)] if dt == 0 else (
                        [(s_s, s_s.n)] if j == 0 else ())
                    ins = p.op("tensor", lambda e, dt=dt, j=j: e.transpose(
                        p_trb[:, j * 128:(j + 1) * 128],
                        xt_bf[:, dt * 128:(dt + 1) * 128], id_bf[:]),
                        waits=wts, inc=s_t if j == 7 else None)
                dst = x1T_t[:].rearrange("p (dt c) -> p dt c", c=128)[
                    :, grp * 8:(grp + 1) * 8, :]
                cp = p.op("scalar", lambda e, dst=dst: e.copy(
                    dst, p_trb[:].rearrange("p (j c) -> p j c", c=128)),
                    waits=[(s_t, ins)], inc=s_s)
            m_xt = cp
            m_trl[t] = s_t.n
            for qc in range(2):
                for dt in range(32):
                    w = []
                    if dt == 0:
                        w = [(s_s, m_xt), (s_v, s_v.n)]
                    ins = p.op("tensor", lambda e, dt=dt, qc=qc: e.matmul(
                        p_q[:], x1T_t[:, dt * 128:(dt + 1) * 128],
                        WqselT[:, dt * 1024 + qc * 512:dt * 1024 + qc * 512 + 512],
                        start=(dt == 0), stop=(dt == 31)),
                        waits=w, inc=s_t if dt == 31 else None)
                t_q = ins
                v_qb = p.op("vector", lambda e, qc=qc: e.tensor_add(
                    qb_f[:, qc * 512:(qc + 1) * 512], p_q[:, 0:512],
                    bq_b[:, qc * 512:(qc + 1) * 512]),
                    waits=[(s_t, t_q), (s_s, c_bq)], inc=s_v)
            v_qm = p.op("vector", lambda e: e.tensor_scalar(
                qm_f[:], qb_f[:], 0.0, None, op0=ALU.min), inc=s_v)
            s_e1 = p.op("scalar", lambda e: e.activation(e1_f[:], qm_f[:], AF.Exp),
                        waits=[(s_v, v_qm)], inc=s_s)
            v_qr = p.op("vector", lambda e: e.tensor_relu(qm_f[:], qb_f[:]),
                        waits=[(s_s, s_e1)])
            v_sg = p.op("vector", lambda e: e.tensor_add(sig_bf[:], e1_f[:], qm_f[:]),
                        waits=[(s_s, s_e1), (s_v, v_qr)], inc=s_v)
            for j in range(8):
                wts = [(s_v, v_sg), (s_s, s_s.n)] if j == 0 else ()
                ins = p.op("tensor", lambda e, j=j: e.transpose(
                    p_trb[:, j * 128:(j + 1) * 128],
                    sig_bf[:, j * 128:(j + 1) * 128], id_bf[:]),
                    waits=wts, inc=s_t if j == 7 else None)
            c_sT = p.op("scalar", lambda e: e.copy(sigT[:], p_trb[:]),
                        waits=[(s_t, ins)], inc=s_s)
            for h in range(8):
                w = [(s_s, c_sT)] if h == 0 else []
                if h == 0:
                    w.append((s_v, s_v.n))
                p.op("tensor", lambda e, h=h: e.matmul(
                    p_n[:, h * 128:(h + 1) * 128], sigT[:, h * 128:(h + 1) * 128],
                    M_bf[:, h * 128:(h + 1) * 128], start=True, stop=True), waits=w)
                ins = p.op("tensor", lambda e, h=h: e.matmul(
                    p_d[:, h:h + 1], sigT[:, h * 128:(h + 1) * 128],
                    zT_bf[:, h:h + 1], start=True, stop=True),
                    inc=s_t if h == 7 else None)
            t_nd = ins
            v_de = p.op("vector", lambda e: e.tensor_scalar(
                den_f[:], p_d[:], 1e-6, None, op0=ALU.add),
                waits=[(s_t, t_nd)], inc=s_v)
            v_rc = p.op("vector", lambda e: e.reciprocal(rcp8[:], den_f[:]),
                        waits=[(s_v, v_de)], inc=s_v)
            for h in range(8):
                ins = p.op("vector", lambda e, h=h: e.tensor_scalar(
                    retr[:, h * 128:(h + 1) * 128], p_n[:, h * 128:(h + 1) * 128],
                    rcp8[:, h:h + 1], None, op0=ALU.mult),
                    waits=[(s_v, v_rc)] if h == 0 else (),
                    inc=s_v if h == 7 else None)
            v_re = ins
            for j in range(8):
                wts = [(s_v, v_re), (s_s, s_s.n)] if j == 0 else ()
                ins = p.op("tensor", lambda e, j=j: e.transpose(
                    p_trb[:, j * 128:(j + 1) * 128],
                    retr[:, j * 128:(j + 1) * 128], id_bf[:]),
                    waits=wts, inc=s_t if j == 7 else None)
            c_rT = p.op("scalar", lambda e: e.copy(retrT[:], p_trb[:]),
                        waits=[(s_t, ins)], inc=s_s)
            b = t % 2
            for oc in range(8):
                for h in range(8):
                    w = []
                    if h == 0:
                        w = [(s_s, c_rT)]
                        if oc == 0 and t >= 2:
                            w.append((s_st, 16 * (t - 1)))
                        if oc > 0:
                            w.append((s_s, s_s.n))
                    ins = p.op("tensor", lambda e, oc=oc, h=h: e.matmul(
                        p_o[:], retrT[:, h * 128:(h + 1) * 128],
                        WoF[:, h * D + oc * 512:h * D + (oc + 1) * 512],
                        start=(h == 0), stop=(h == 7)),
                        waits=w, inc=s_t if h == 7 else None)
                w2 = [(s_t, ins)]
                if oc == 0 and t >= 1:
                    w2.append((s_st, 16 * t))
                c_o = p.op("scalar", lambda e, b=b, oc=oc: e.copy(
                    ao_s[b][:, oc * 512:(oc + 1) * 512], p_o[:]),
                    waits=w2, inc=s_s)
            p.op("gpsimd", lambda e, t=t, b=b: e.dma_start(
                out=attn1[t * 128:(t + 1) * 128, :], in_=ao_s[b][:]),
                waits=[(s_s, c_o)], inc=s_st, inck=16)

        p.op("gpsimd", lambda e: e.nop(), waits=[(s_st, 16 * NT)])


def build_attn1(NT=16):
    nc = bass.Bass("TRN2", target_bir_lowering=False, debug=False)
    Ntok = NT * 128
    xn1 = nc.dram_tensor("xn1", [Ntok, D], BF16, kind="ExternalInput").ap()
    Wq_sel = nc.dram_tensor("Wq_sel", [1024, D], F32, kind="ExternalInput").ap()
    bq1 = nc.dram_tensor("bq1", [1, 1024], F32, kind="ExternalInput").ap()
    Wo = nc.dram_tensor("Wo", [D, D], F32, kind="ExternalInput").ap()
    Mm = nc.dram_tensor("Mm", [1024, 128], F32, kind="ExternalInput").ap()
    zT8 = nc.dram_tensor("zT8", [8, 128], F32, kind="ExternalInput").ap()
    ident = nc.dram_tensor("ident", [128, 128], BF16, kind="ExternalInput").ap()
    attn1 = nc.dram_tensor("attn1", [Ntok, D], F32, kind="ExternalOutput").ap()
    p = Prog()
    with ExitStack() as ctx:
        SP = SemPool(ctx, nc)
        emit_attn1(nc, p, SP, NT, xn1, Wq_sel, bq1, Wo, Mm, zT8, ident, attn1)
        block = ctx.enter_context(nc.Block())
        p.emit(block)
    return nc


def run_attn1(xn1, Wq, bq, Wo, M, z):
    Ntok_all = xn1.shape[0] * xn1.shape[1]
    ntok = Ntok_all // 8
    nc = build_attn1(ntok // 128)
    sel = np.concatenate([np.arange(512 * h, 512 * h + 128) for h in range(8)])
    Wq_sel = np.ascontiguousarray(Wq[sel])
    bq_sel = np.ascontiguousarray(bq[sel]).reshape(1, 1024)
    xs = xn1.reshape(8, ntok, D)
    in_maps = []
    for r in range(8):
        in_maps.append(dict(
            xn1=np.ascontiguousarray(xs[r]).astype(BF, copy=False),
            Wq_sel=Wq_sel, bq1=bq_sel, Wo=Wo,
            Mm=np.ascontiguousarray(M.reshape(1024, 128)),
            zT8=np.ascontiguousarray(z), ident=_IDENT))
    res = run_bass_kernel_spmd(nc, in_maps, core_ids=list(range(8)))
    out = np.concatenate([res.results[r]["attn1"] for r in range(8)], 0)
    return out.reshape(xn1.shape[0], xn1.shape[1], D)




def emit_ffn1(nc, p, SP, NT, xn1, x1o, a1o, qwgu, qzgu, scgu, qwdn, qzdn,
              scdn, ident, sel, out, hbuf, wdnbuf, pre_waits=()):
    Ntok = NT * 128
    TCH = 512 if Ntok % 512 == 0 else Ntok
    NTCH = Ntok // TCH
    TC = 1024 if Ntok % 1024 == 0 else Ntok
    NTC = Ntok // TC
    INTER = 11008
    NCH = 43
    NIB = 86
    sb, ps = _pfx(nc, "D_")
    with ExitStack() as ctx:
        E = ctx.enter_context
        id_bf = E(sb("id_bf", [128, 128], BF16))
        ones1 = E(sb("ones1", [1, 128], F32))
        s_set = SP("D_set")
        s_x = SP("D_x")
        s_qw = SP("D_qw")
        s_scp = [SP(f"D_scp{i}") for i in range(2)]
        Sh = [SP(f"D_h{i}") for i in range(2)]
        s_v = SP("s_v")
        s_s = SP("s_s")
        s_t = SP("s_t")
        p.serialize(vector=s_v, scalar=s_s)

        with ExitStack() as d1:
            E1 = d1.enter_context
            sel_s = E1(sb("sel_s", [32, 32 * 128], BF16))
            p.op("sync", lambda e: e.dma_start(out=id_bf[:], in_=ident),
                 waits=list(pre_waits), inc=s_set, inck=16)
            p.op("sync", lambda e: e.dma_start(out=sel_s[:], in_=sel), inc=s_set, inck=16)
            ld_set = s_set.n
            v_o1 = p.op("vector", lambda e: e.memset(ones1[:], 1.0), waits=[(s_set, ld_set)])
            xn1T = E1(sb("xn1T", [128, 32 * Ntok], BF16))
            xt_bf = E1(sb("xt_bf", [128, D], BF16))
            qzc_s = E1(sb("qzc_s", [32, 32], I32))
            Wg = E1(sb("Wg", [128, 32 * 256], BF16))
            Wu = E1(sb("Wu", [128, 32 * 256], BF16))
            qwst = [E1(sb(f"qwst{i}", [128, 1024], I32)) for i in range(2)]
            wi32 = E1(sb("wi32", [128, 8 * 256], I32))
            scch = [E1(sb(f"scch{i}", [32, 256], F32)) for i in range(2)]
            scbf = [E1(sb(f"scbf{i}", [32, 256], BF16)) for i in range(2)]
            zi_s = E1(sb("zi_s", [32, 256], I32))
            zbf_s = E1(sb("zbf_s", [32, 256], BF16))
            wd_f = E1(sb("wd_f", [128, 512], F32))
            g_act = E1(sb("g_act", [128, TCH], F32))
            hT_bf = [E1(sb(f"hT_bf{i}", [128, TCH], BF16)) for i in range(2)]
            p_trb = E1(ps("p_trb", [128, 8 * 128], BF16))
            p_sb = [E1(ps(f"p_sb{i}", [128, 256], F32)) for i in range(2)]
            p_zb = [E1(ps(f"p_zb{i}", [128, 256], F32)) for i in range(2)]
            p_g = E1(ps("p_g", [128, TCH], F32))
            p_u = E1(ps("p_u", [128, TCH], F32))

            m_trl = {}
            for t in range(NT):
                w = [] if t == 0 else [(s_t, m_trl[t - 1])]
                ldx = p.op("sync", lambda e, t=t: e.dma_start(
                    out=xt_bf[:], in_=xn1[t * 128:(t + 1) * 128, :]),
                    waits=w, inc=s_x, inck=16)
                for grp in range(4):
                    for j in range(8):
                        dt = grp * 8 + j
                        wts = [(s_x, ldx), (s_set, ld_set), (s_s, s_s.n)] if dt == 0 else (
                            [(s_s, s_s.n)] if j == 0 else ())
                        ins = p.op("tensor", lambda e, dt=dt, j=j: e.transpose(
                            p_trb[:, j * 128:(j + 1) * 128],
                            xt_bf[:, dt * 128:(dt + 1) * 128], id_bf[:]),
                            waits=wts, inc=s_t if j == 7 else None)
                    dst = xn1T[:].rearrange("p (dt c) -> p dt c", c=Ntok)[
                        :, grp * 8:(grp + 1) * 8, t * 128:(t + 1) * 128]
                    cp = p.op("scalar", lambda e, dst=dst: e.copy(
                        dst, p_trb[:].rearrange("p (j c) -> p j c", c=128)),
                        waits=[(s_t, ins)])
                m_trl[t] = s_t.n
            m_xnt = s_s.n

            hs_cnt = [0, 0]
            dqmult = [None, None]
            for c in range(NCH):
                halves = []
                for half, Wbuf in enumerate([Wg, Wu]):
                    cb = (2 * c + half) % 2
                    pc0 = half * 1376 + c * 32
                    ic0 = half * INTER + c * 256
                    w = []
                    if c > 0 or half > 0:
                        w = [(s_v, m_unp)]
                    for dt in range(32):
                        ldw = p.op("sync", lambda e, dt=dt, pc0=pc0, cb=cb: e.dma_start(
                            out=qwst[cb][:, dt * 32:(dt + 1) * 32],
                            in_=qwgu[dt * 128:(dt + 1) * 128, pc0:pc0 + 32]),
                            waits=w if dt == 0 else (), inc=s_qw, inck=16)
                    p.op("sync", lambda e, ic0=ic0, cb=cb: e.dma_start(
                        out=scch[cb][:], in_=scgu[:, ic0:ic0 + 256]),
                        inc=s_scp[cb], inck=16)
                    ldsc = p.op("sync", lambda e, pc0=pc0: e.dma_start(
                        out=qzc_s[:], in_=qzgu[:, pc0:pc0 + 32]),
                        inc=s_scp[cb], inck=16)
                    w = [(s_scp[cb], ldsc)]
                    p.op("vector", lambda e, cb=cb: e.tensor_copy(
                        scbf[cb][:], scch[cb][:]), waits=w)
                    for nib in range(8):
                        dst = zi_s[:].rearrange("p (j n) -> p j n", n=8)[:, :, nib]
                        ins = p.op("vector", lambda e, nib=nib, dst=dst:
                                   e.tensor_scalar(dst, qzc_s[:],
                                                   4 * nib, 15,
                                                   op0=ALU.logical_shift_right,
                                                   op1=ALU.bitwise_and),
                                   waits=w if nib == 0 else ())
                    m_prep1 = p.op("vector", lambda e: e.tensor_copy(zbf_s[:], zi_s[:]))
                    for rnd in range(4):
                        w = [(s_qw, s_qw.n)]
                        if c > 0 or half > 0 or rnd > 0:
                            w.append((s_v, s_v.n))
                        for nib in range(8):
                            src = qwst[cb][:].rearrange(
                                "p (dt j) -> p dt j", j=32)[:, rnd * 8:(rnd + 1) * 8, :]
                            dst = wi32[:].rearrange(
                                "p (dt j n) -> p dt j n", j=32, n=8)[:, :, :, nib]
                            ins = p.op("vector", lambda e, src=src, dst=dst, nib=nib:
                                       e.tensor_scalar(dst, src, 4 * nib, 15,
                                                       op0=ALU.logical_shift_right,
                                                       op1=ALU.bitwise_and),
                                       waits=w if nib == 0 else ())
                        m_unp = ins
                        for k in range(8):
                            dt = rnd * 8 + k
                            par = dt % 2
                            w0 = [(s_v, m_prep1)]
                            if dqmult[par]:
                                w0.append((s_v, dqmult[par]))
                            p.op("tensor", lambda e, dt=dt, par=par: e.matmul(
                                p_zb[par][:],
                                sel_s[:, dt * 128:(dt + 1) * 128],
                                zbf_s[:], start=True, stop=True), waits=w0)
                            t_sb = p.op("tensor", lambda e, dt=dt, cb=cb, par=par: e.matmul(
                                p_sb[par][:],
                                sel_s[:, dt * 128:(dt + 1) * 128],
                                scbf[cb][:], start=True, stop=True), inc=s_t)
                            p.op("vector",
                                 lambda e, k=k, par=par: e.tensor_tensor(
                                     wd_f[:, par * 256:(par + 1) * 256],
                                     wi32[:, k * 256:(k + 1) * 256],
                                     p_zb[par][:],
                                     op=ALU.subtract),
                                 waits=[(s_t, t_sb)])
                            dqmult[par] = p.op(
                                "vector",
                                lambda e, dt=dt, Wbuf=Wbuf, par=par: e.tensor_tensor(
                                    Wbuf[:, dt * 256:(dt + 1) * 256],
                                    wd_f[:, par * 256:(par + 1) * 256],
                                    p_sb[par][:], op=ALU.mult))
                            m_unp = dqmult[par]
                    halves.append(m_unp)
                m_wg, m_wu = halves[0], halves[1]

                for tch in range(NTCH):
                    for isub in range(2):
                        gi = c * 256 + isub * 128
                        hb = (tch * 2 + isub) % 2
                        w = [(s_v, s_v.n), (s_s, s_s.n)]
                        for dt in range(32):
                            ins = p.op("tensor", lambda e, dt=dt, isub=isub, tch=tch:
                                       e.matmul(
                                           p_g[:],
                                           Wg[:, dt * 256 + isub * 128:dt * 256 + isub * 128 + 128],
                                           xn1T[:, dt * Ntok + tch * TCH:dt * Ntok + (tch + 1) * TCH],
                                           start=(dt == 0), stop=(dt == 31)),
                                       waits=w if dt == 0 else ())
                        for dt in range(32):
                            ins = p.op("tensor", lambda e, dt=dt, isub=isub, tch=tch:
                                       e.matmul(
                                           p_u[:],
                                           Wu[:, dt * 256 + isub * 128:dt * 256 + isub * 128 + 128],
                                           xn1T[:, dt * Ntok + tch * TCH:dt * Ntok + (tch + 1) * TCH],
                                           start=(dt == 0), stop=(dt == 31)),
                                       inc=s_t if dt == 31 else None)
                        t_gu = ins
                        s_si = p.op("scalar", lambda e: e.activation(
                            g_act[:], p_g[:], AF.Silu), waits=[(s_t, t_gu)])
                        w2 = [(s_s, s_si)]
                        if hs_cnt[hb] > 0:
                            w2.append((Sh[hb], 16 * hs_cnt[hb]))
                        m_h = p.op("vector", lambda e, hb=hb: e.tensor_tensor(
                            hT_bf[hb][:], g_act[:], p_u[:], op=ALU.mult), waits=w2)
                        p.op("gpsimd", lambda e, gi=gi, tch=tch, hb=hb: e.dma_start(
                            out=hbuf[gi:gi + 128, tch * TCH:(tch + 1) * TCH],
                            in_=hT_bf[hb][:]),
                            waits=[(s_v, m_h)], inc=Sh[hb], inck=16)
                        hs_cnt[hb] += 1
            d1_t, d1_v, d1_s = s_t.n, s_v.n, s_s.n

        s_wb = SP("D_wb")
        with ExitStack() as d2a:
            E2 = d2a.enter_context
            Wdq = [E2(sb(f"Wdq{i}", [128, D], BF16)) for i in range(2)]
            qwst2 = [E2(sb(f"qwst2{i}", [128, 512], I32)) for i in range(2)]
            wi2 = E2(sb("wi2", [128, 1024], I32))
            wd_f2 = E2(sb("wd_f2", [128, 1024], F32))
            scr_s = [E2(sb(f"scr_s{i}", [1, D], F32)) for i in range(2)]
            scr_bf = [E2(sb(f"scr_bf{i}", [1, D], BF16)) for i in range(2)]
            qzr_s = [E2(sb(f"qzr_s{i}", [1, 512], I32)) for i in range(2)]
            zr_i = E2(sb("zr_i", [1, D], I32))
            zr_bf = [E2(sb(f"zr_bf{i}", [1, D], BF16)) for i in range(2)]
            ones_bf = E2(sb("ones_bf", [1, 128], BF16))
            p_zbA = [E2(ps(f"p2_zb{i}", [128, 1024], F32)) for i in range(2)]
            p_sbA = [E2(ps(f"p2_sb{i}", [128, 1024], F32)) for i in range(2)]
            s_q2 = [SP(f"D_q2{i}") for i in range(2)]
            s_r2 = [SP(f"D_r2{i}") for i in range(2)]
            unpk_mark = [0, 0]
            zrow_mark = [0, 0]
            brow_mark = [0, 0]
            pz_cons = [0, 0]
            ps_cons = [0, 0]
            r2_cnt = [0, 0]
            p.op("vector", lambda e: e.memset(ones_bf[:], 1.0),
                 waits=[(Sh[0], 16 * hs_cnt[0]), (Sh[1], 16 * hs_cnt[1]),
                        (s_t, d1_t), (s_s, d1_s)])
            for ib in range(NIB):
                par = ib % 2
                if ib < 2:
                    wq = [(s_t, d1_t), (s_v, d1_v), (s_s, d1_s),
                          (Sh[0], 16 * hs_cnt[0]), (Sh[1], 16 * hs_cnt[1])]
                    wz = []
                else:
                    wq = [(s_v, unpk_mark[par])]
                    wz = [(s_v, zrow_mark[par])]
                ldq = p.op("sync", lambda e, ib=ib, par=par: e.dma_start(
                    out=qwst2[par][:], in_=qwdn[ib * 128:(ib + 1) * 128, :]),
                    waits=wq, inc=s_q2[par], inck=16)
                p.op("sync", lambda e, ib=ib, par=par: e.dma_start(
                    out=qzr_s[par][:], in_=qzdn[ib:ib + 1, :]),
                    waits=wz, inc=s_r2[par], inck=16)
                r2_cnt[par] += 1
                ldr = p.op("sync", lambda e, ib=ib, par=par: e.dma_start(
                    out=scr_s[par][:], in_=scdn[ib:ib + 1, :]),
                    inc=s_r2[par], inck=16)
                r2_cnt[par] += 1
                w0 = [(s_r2[par], ldr)]
                if ib >= 2:
                    w0.append((s_t, brow_mark[par]))
                p.op("vector", lambda e, par=par: e.tensor_copy(
                    scr_bf[par][:], scr_s[par][:]), waits=w0)
                for nib in range(8):
                    dst = zr_i[:].rearrange("p (j n) -> p j n", n=8)[:, :, nib]
                    p.op("vector", lambda e, dst=dst, nib=nib, par=par:
                         e.tensor_scalar(dst, qzr_s[par][:], 4 * nib, 15,
                                         op0=ALU.logical_shift_right,
                                         op1=ALU.bitwise_and))
                m_zprep = p.op("vector", lambda e, par=par: e.tensor_copy(
                    zr_bf[par][:], zr_i[:]))
                zrow_mark[par] = m_zprep
                for oh in range(4):
                    php = oh % 2
                    wb1 = [(s_v, m_zprep)]
                    if pz_cons[php]:
                        wb1.append((s_v, pz_cons[php]))
                    if ps_cons[php]:
                        wb1.append((s_v, ps_cons[php]))
                    for q in range(2):
                        p.op("tensor", lambda e, oh=oh, par=par, php=php, q=q:
                             e.matmul(
                                 p_zbA[php][:, q * 512:(q + 1) * 512], ones_bf[:],
                                 zr_bf[par][:, oh * 1024 + q * 512:
                                             oh * 1024 + (q + 1) * 512],
                                 start=True, stop=True),
                             waits=wb1 if q == 0 else ())
                        t_sb = p.op("tensor", lambda e, oh=oh, par=par, php=php,
                                    q=q: e.matmul(
                                        p_sbA[php][:, q * 512:(q + 1) * 512],
                                        ones_bf[:],
                                        scr_bf[par][:, oh * 1024 + q * 512:
                                                     oh * 1024 + (q + 1) * 512],
                                        start=True, stop=True),
                                    inc=s_t if q == 1 else None)
                    if oh == 3:
                        brow_mark[par] = t_sb
                    for nib in range(8):
                        src = qwst2[par][:, oh * 128:(oh + 1) * 128]
                        dst = wi2[:].rearrange("p (j n) -> p j n", n=8)[:, :, nib]
                        wun = [(s_q2[par], ldq)] if (oh == 0 and nib == 0) else ()
                        p.op("vector", lambda e, src=src, dst=dst, nib=nib:
                             e.tensor_scalar(dst, src, 4 * nib, 15,
                                             op0=ALU.logical_shift_right,
                                             op1=ALU.bitwise_and), waits=wun)
                    if oh == 3:
                        unpk_mark[par] = s_v.n
                    pz_cons[php] = p.op("vector", lambda e, php=php: e.tensor_tensor(
                        wd_f2[:], wi2[:], p_zbA[php][:], op=ALU.subtract),
                        waits=[(s_t, t_sb)])
                    wmul = []
                    if oh == 0 and ib >= 2:
                        wmul.append((s_wb, 16 * (ib - 1)))
                    m_mult = p.op("vector", lambda e, oh=oh, par=par, php=php:
                                  e.tensor_tensor(
                                      Wdq[par][:, oh * 1024:(oh + 1) * 1024],
                                      wd_f2[:], p_sbA[php][:], op=ALU.mult),
                                  waits=wmul)
                    ps_cons[php] = m_mult
                p.op("gpsimd", lambda e, ib=ib, par=par: e.dma_start(
                    out=wdnbuf[ib * 128:(ib + 1) * 128, :], in_=Wdq[par][:]),
                    waits=[(s_v, m_mult)], inc=s_wb, inck=16)
            d2a_v = s_v.n

        TCB = 512
        NTCB = Ntok // TCB
        with ExitStack() as d2b:
            E3 = d2b.enter_context
            hT_all = E3(sb("hT_all", [128, NIB * TCB], BF16))
            Wdb = [E3(sb(f"Wdb{i}", [128, 512], BF16)) for i in range(4)]
            fx_s = [E3(sb(f"fx_s{i}", [128, 512], F32)) for i in range(2)]
            fa_s = [E3(sb(f"fa_s{i}", [128, 512], F32)) for i in range(2)]
            fo_s = [E3(sb(f"fo_s{i}", [128, 512], F32)) for i in range(2)]
            p_do = [E3(ps(f"p_do{i}", [128, 512], F32)) for i in range(8)]
            s_ht = [SP("D_ht0"), SP("D_ht1")]
            s_wds = [SP(f"D_wds{i}") for i in range(4)]
            s_fin2 = [SP(f"D_fin{i}") for i in range(2)]
            s_st2 = [SP(f"D_st{i}") for i in range(2)]
            ht_q = [0, 0]
            slot_use = [0, 0, 0, 0]
            fin_p = [0, 0]
            st_p = [0, 0]
            wd_cnt = 0
            wd_mark = {}
            oc7_mark = {}
            evac_mark = [0, 0]
            fx_mark = [0, 0]
            fo_store = [0, 0]
            for tc in range(NTCB):
                prev_oc7 = dict(oc7_mark)
                cur_ht = {}
                for ib in range(NIB):
                    w = []
                    if tc == 0:
                        if ib == 0:
                            w = [(s_wb, 16 * NIB)]
                    else:
                        w = [(s_t, prev_oc7[ib])]
                    qi = ib % 2
                    eng = "sync" if qi == 0 else "gpsimd"
                    p.op(eng, lambda e, ib=ib, tc=tc: e.dma_start(
                        out=hT_all[:, ib * TCB:(ib + 1) * TCB],
                        in_=hbuf[ib * 128:(ib + 1) * 128, tc * TCB:(tc + 1) * TCB]),
                        waits=w, inc=s_ht[qi], inck=16)
                    ht_q[qi] += 1
                for oc in range(8):
                    par = oc % 2
                    for ib in range(NIB):
                        bslot = wd_cnt % 4
                        w = []
                        if wd_cnt >= 4:
                            w.append((s_t, wd_mark[wd_cnt - 4]))
                        if wd_cnt == 0:
                            w.append((s_wb, 16 * NIB))
                        qi = wd_cnt % 2
                        eng = "sync" if qi == 0 else "gpsimd"
                        slot_use[bslot] += 1
                        ld = (bslot, 16 * slot_use[bslot])
                        p.op(eng, lambda e, ib=ib, oc=oc, bslot=bslot: e.dma_start(
                            out=Wdb[bslot][:],
                            in_=wdnbuf[ib * 128:(ib + 1) * 128,
                                       oc * 512:(oc + 1) * 512]),
                            waits=w, inc=s_wds[bslot], inck=16)
                        for tt in range(4):
                            w2 = []
                            if tt == 0:
                                w2 = [(s_wds[ld[0]], ld[1])]
                                if oc == 0 and ib == 0:
                                    w2.append((s_ht[0], 16 * ht_q[0]))
                                    w2.append((s_ht[1], 16 * ht_q[1]))
                                if ib == 0:
                                    w2.append((s_v, max(evac_mark[par], d2a_v)))
                            p.op("tensor", lambda e, ib=ib, tt=tt, par=par,
                                 bslot=bslot: e.matmul(
                                     p_do[par * 4 + tt][:],
                                     hT_all[:, ib * TCB + tt * 128:
                                            ib * TCB + (tt + 1) * 128],
                                     Wdb[bslot][:],
                                     start=(ib == 0), stop=(ib == NIB - 1)),
                                 waits=w2, inc=s_t if tt == 3 else None)
                        wd_mark[wd_cnt] = s_t.n
                        wd_cnt += 1
                        if oc == 7:
                            oc7_mark[ib] = s_t.n
                    oc_end = s_t.n
                    for tt in range(4):
                        fpar = tt % 2
                        r0 = tc * TCB + tt * 128
                        cc = oc * 512
                        wl = [(s_v, fx_mark[fpar])] if fx_mark[fpar] else []
                        p.op("sync", lambda e, r0=r0, cc=cc, fpar=fpar: e.dma_start(
                            out=fx_s[fpar][:], in_=x1o[r0:r0 + 128, cc:cc + 512]),
                            waits=wl, inc=s_fin2[fpar], inck=16)
                        p.op("sync", lambda e, r0=r0, cc=cc, fpar=fpar: e.dma_start(
                            out=fa_s[fpar][:], in_=a1o[r0:r0 + 128, cc:cc + 512]),
                            inc=s_fin2[fpar], inck=16)
                        fin_p[fpar] += 2
                        w3 = [(s_fin2[fpar], 16 * fin_p[fpar]), (s_t, oc_end)]
                        if fo_store[fpar]:
                            w3.append((s_st2[fpar], fo_store[fpar]))
                        p.op("vector", lambda e, tt=tt, par=par, fpar=fpar:
                             e.tensor_add(fo_s[fpar][:], fx_s[fpar][:],
                                          p_do[par * 4 + tt][:]), waits=w3)
                        m_fo = p.op("vector", lambda e, fpar=fpar: e.tensor_add(
                            fo_s[fpar][:], fo_s[fpar][:], fa_s[fpar][:]))
                        fx_mark[fpar] = m_fo
                        p.op("gpsimd", lambda e, r0=r0, cc=cc, fpar=fpar: e.dma_start(
                            out=out[r0:r0 + 128, cc:cc + 512], in_=fo_s[fpar][:]),
                            waits=[(s_v, m_fo)], inc=s_st2[fpar], inck=16)
                        st_p[fpar] += 1
                        fo_store[fpar] = 16 * st_p[fpar]
                    evac_mark[par] = s_v.n

            p.op("gpsimd", lambda e: e.nop(),
                 waits=[(s_st2[0], 16 * st_p[0]), (s_st2[1], 16 * st_p[1])])


def build_ffn1(NT=16):
    nc = bass.Bass("TRN2", target_bir_lowering=False, debug=False)
    Ntok = NT * 128
    INTER = 11008
    xn1 = nc.dram_tensor("xn1", [Ntok, D], BF16, kind="ExternalInput").ap()
    x1o = nc.dram_tensor("x1o", [Ntok, D], F32, kind="ExternalInput").ap()
    a1o = nc.dram_tensor("a1o", [Ntok, D], F32, kind="ExternalInput").ap()
    qwgu = nc.dram_tensor("qwgu", [D, 2752], I32, kind="ExternalInput").ap()
    qzgu = nc.dram_tensor("qzgu", [32, 2752], I32, kind="ExternalInput").ap()
    scgu = nc.dram_tensor("scgu", [32, 2 * INTER], F32, kind="ExternalInput").ap()
    qwdn = nc.dram_tensor("qwdn", [INTER, 512], I32, kind="ExternalInput").ap()
    qzdn = nc.dram_tensor("qzdn", [86, 512], I32, kind="ExternalInput").ap()
    scdn = nc.dram_tensor("scdn", [86, D], F32, kind="ExternalInput").ap()
    ident = nc.dram_tensor("ident", [128, 128], BF16, kind="ExternalInput").ap()
    ind = nc.dram_tensor("ind", [128, 32 * 32], BF16, kind="ExternalInput").ap()
    sel = nc.dram_tensor("sel", [32, 32 * 128], BF16, kind="ExternalInput").ap()
    out = nc.dram_tensor("out", [Ntok, D], F32, kind="ExternalOutput").ap()
    hbuf = nc.dram_tensor("hbuf", [INTER, Ntok], BF16, kind="Internal").ap()
    wdnbuf = nc.dram_tensor("wdnbuf", [INTER, D], BF16, kind="Internal").ap()
    p = Prog()
    with ExitStack() as ctx:
        SP = SemPool(ctx, nc)
        emit_ffn1(nc, p, SP, NT, xn1, x1o, a1o, qwgu, qzgu, scgu, qwdn, qzdn,
                  scdn, ident, sel, out, hbuf, wdnbuf)
        block = ctx.enter_context(nc.Block())
        p.emit(block)
    return nc


_IND = np.zeros((128, 32 * 32), dtype=BF)
for _dt in range(32):
    _IND[:, _dt * 32 + _dt] = 1
_SEL = np.zeros((32, 32 * 128), dtype=BF)
for _g in range(32):
    _SEL[_g, _g * 128:(_g + 1) * 128] = 1


def run_ffn1(xn1, x1, attn1, qw_gu, qz_gu, sc_gu, qw_dn, qz_dn, sc_dn):
    B, T, _ = xn1.shape
    ntok = B * T // 8
    nc = build_ffn1(ntok // 128)
    xs = xn1.reshape(8, ntok, D)
    x1s = x1.reshape(8, ntok, D)
    a1s = attn1.reshape(8, ntok, D)
    in_maps = []
    for r in range(8):
        in_maps.append(dict(
            xn1=np.ascontiguousarray(xs[r]).astype(BF, copy=False),
            x1o=np.ascontiguousarray(x1s[r]), a1o=np.ascontiguousarray(a1s[r]),
            qwgu=qw_gu, qzgu=qz_gu, scgu=sc_gu, qwdn=qw_dn, qzdn=qz_dn,
            scdn=sc_dn, ident=_IDENT, ind=_IND, sel=_SEL))
    res = run_bass_kernel_spmd(nc, in_maps, core_ids=list(range(8)))
    o = np.concatenate([res.results[r]["out"] for r in range(8)], 0)
    return o.reshape(B, T, D)



CC_GROUPS = [[0, 1, 2, 3], [4, 5, 6, 7]]


def build_fused(T=8192):
    NTOK = T // 4
    NT = NTOK // 128
    INTER = 11008
    nc = bass.Bass("TRN2", target_bir_lowering=False, debug=False)
    t_in = lambda n, sh, dt: nc.dram_tensor(n, sh, dt, kind="ExternalInput").ap()
    xb = t_in("xb", [T, D], F32)
    bvq8 = t_in("bvq8", [8, 128], F32)
    bvk8 = t_in("bvk8", [8, 128], F32)
    bvv1 = t_in("bvv1", [1, HD0], F32)
    ident = t_in("ident", [128, 128], BF16)
    identf = t_in("identf", [128, 128], F32)
    triu = t_in("triu", [128, 128], BF16)
    xo = t_in("xo", [NTOK, D], F32)
    A_w = t_in("A_w", [512, D], F32)
    B_w = t_in("B_w", [D, 512], F32)
    Wq_w = t_in("Wq_w", [16, D], F32)
    keys = t_in("keys", [64, 16], F32)
    a0idx = t_in("a0idx", [128, 4 * NT], I32)
    Wq_sel = t_in("Wq_sel", [1024, D], F32)
    bq1 = t_in("bq1", [1, 1024], F32)
    Wo = t_in("Wo", [D, D], F32)
    Mm = t_in("Mm", [1024, 128], F32)
    zT8 = t_in("zT8", [8, 128], F32)
    qwgu = t_in("qwgu", [D, 2752], I32)
    qzgu = t_in("qzgu", [32, 2752], I32)
    scgu = t_in("scgu", [32, 2 * INTER], F32)
    qwdn = t_in("qwdn", [INTER, 512], I32)
    qzdn = t_in("qzdn", [86, 512], I32)
    scdn = t_in("scdn", [86, D], F32)
    sel = t_in("sel", [32, 32 * 128], BF16)
    out = nc.dram_tensor("out", [NTOK, D], F32, kind="ExternalOutput").ap()
    vbuf = nc.dram_tensor("vbuf", [T, HD0], BF16, kind="Internal").ap()
    a0loc = nc.dram_tensor("a0loc", [T, HD0], F32, kind="Internal").ap()
    a0g = nc.dram_tensor("a0g", [4 * T, HD0], F32, kind="Internal").ap()
    x1b = nc.dram_tensor("x1b", [NTOK, D], F32, kind="Internal").ap()
    xn1b = nc.dram_tensor("xn1b", [NTOK, D], BF16, kind="Internal").ap()
    a1b = nc.dram_tensor("a1b", [NTOK, D], F32, kind="Internal").ap()
    hbuf = nc.dram_tensor("hbuf", [INTER, NTOK], BF16, kind="Internal").ap()
    wdnbuf = nc.dram_tensor("wdnbuf", [INTER, D], BF16, kind="Internal").ap()

    p = Prog()
    with ExitStack() as ctx:
        SP = SemPool(ctx, nc)
        NC_A = emit_attn0(nc, p, SP, T, xb, bvq8, bvk8, bvv1, ident, identf,
                          triu, a0loc, vbuf)
        s_cc = SP("s_cc")
        p.op("gpsimd", lambda e: e.collective_compute(
            "AllGather", ALU.bypass, CC_GROUPS, [a0loc.opt()], [a0g.opt()]),
            waits=[(SP("A_a0"), 16 * NC_A), (SP("A_a1"), 16 * NC_A)],
            inc=s_cc, inck=1)
        p.barrier(SP, extra=[(s_cc, 1)])
        emit_ffn0(nc, p, SP, NT, xo, None, A_w, B_w, Wq_w, keys, identf,
                  x1b, xn1b, a0g=a0g, a0idx=a0idx)
        p.barrier(SP, extra=[(SP("B_st10"), 16 * ((NT + 1) // 2)),
                             (SP("B_st11"), 16 * (NT // 2)),
                             (SP("B_st2"), 16 * NT)])
        emit_attn1(nc, p, SP, NT, xn1b, Wq_sel, bq1, Wo, Mm, zT8, ident, a1b)
        p.barrier(SP, extra=[(SP("C_st"), 16 * NT)])
        emit_ffn1(nc, p, SP, NT, xn1b, x1b, a1b, qwgu, qzgu, scgu, qwdn, qzdn,
                  scdn, ident, sel, out, hbuf, wdnbuf)
        block = ctx.enter_context(nc.Block())
        p.emit(block)
    return nc


def _fused_in_maps(inputs, T):
    x = np.asarray(inputs["L0_attn_x"])
    NTOK = T // 4
    NT = NTOK // 128
    bv_q = np.asarray(inputs["L0_attn_bv_q"])
    bv_k = np.asarray(inputs["L0_attn_bv_k"])
    bv_v = np.asarray(inputs["L0_attn_bv_v"])
    Wq = np.asarray(inputs["L1_attn_Wq"])
    bq = np.asarray(inputs["L1_attn_bq"])
    selq = np.concatenate([np.arange(512 * h, 512 * h + 128) for h in range(8)])
    Wq_sel = np.ascontiguousarray(Wq[selq])
    bq_sel = np.ascontiguousarray(bq[selq]).reshape(1, 1024)
    xs = x.reshape(8, NTOK, D)
    in_maps = []
    for r in range(8):
        b, h = r // 4, r % 4
        c = r % 4
        idx = np.empty((128, 4 * NT), np.int32)
        for t in range(NT):
            for hh in range(4):
                idx[:, t * 4 + hh] = hh * T + c * NTOK + t * 128 + np.arange(128)
        in_maps.append(dict(
            xb=np.ascontiguousarray(np.roll(x[b], -HD0 * h, axis=1)),
            bvq8=np.ascontiguousarray(bv_q[h].reshape(8, 128)),
            bvk8=np.ascontiguousarray(bv_k[h].reshape(8, 128)),
            bvv1=np.ascontiguousarray(bv_v[h].reshape(1, HD0)),
            ident=_IDENT, identf=_IDENTF, triu=_TRIU,
            xo=np.ascontiguousarray(xs[r]),
            A_w=np.asarray(inputs["L0_ffn_A_w"]),
            B_w=np.asarray(inputs["L0_ffn_B_w"]),
            Wq_w=np.asarray(inputs["L0_ffn_Wq_w"]),
            keys=np.asarray(inputs["L0_ffn_keys"]),
            a0idx=idx, Wq_sel=Wq_sel, bq1=bq_sel,
            Wo=np.asarray(inputs["L1_attn_Wo"]),
            Mm=np.ascontiguousarray(np.asarray(inputs["L1_attn_M"]).reshape(1024, 128)),
            zT8=np.asarray(inputs["L1_attn_z"]),
            qwgu=np.asarray(inputs["L1_ffn_qweight_gu"]),
            qzgu=np.asarray(inputs["L1_ffn_qzeros_gu"]),
            scgu=np.asarray(inputs["L1_ffn_scales_gu"]),
            qwdn=np.asarray(inputs["L1_ffn_qweight_dn"]),
            qzdn=np.asarray(inputs["L1_ffn_qzeros_dn"]),
            scdn=np.asarray(inputs["L1_ffn_scales_dn"]),
            sel=_SEL,
        ))
    return in_maps


def kernel_fused(inputs, T=8192):
    x = np.asarray(inputs["L0_attn_x"])
    nc = build_fused(T)
    in_maps = _fused_in_maps(inputs, T)
    res = run_bass_kernel_spmd(nc, in_maps, core_ids=list(range(8)))
    o = np.concatenate([res.results[r]["out"] for r in range(8)], 0)
    return o.reshape(x.shape[0], x.shape[1], D)


def kernel(**inputs):
    _ = inputs["L1_attn_Wk"], inputs["L1_attn_Wv"]
    return kernel_fused(inputs, T=np.asarray(inputs["L0_attn_x"]).shape[1])


def kernel_unfused(**inputs):
    x = np.asarray(inputs["L0_attn_x"])
    attn0 = run_attn0(x, np.asarray(inputs["L0_attn_bv_q"]),
                      np.asarray(inputs["L0_attn_bv_k"]),
                      np.asarray(inputs["L0_attn_bv_v"]))
    x1, xn1 = run_ffn0(x, attn0, np.asarray(inputs["L0_ffn_A_w"]),
                       np.asarray(inputs["L0_ffn_B_w"]),
                       np.asarray(inputs["L0_ffn_Wq_w"]),
                       np.asarray(inputs["L0_ffn_keys"]))
    attn1 = run_attn1(xn1, np.asarray(inputs["L1_attn_Wq"]),
                      np.asarray(inputs["L1_attn_bq"]),
                      np.asarray(inputs["L1_attn_Wo"]),
                      np.asarray(inputs["L1_attn_M"]),
                      np.asarray(inputs["L1_attn_z"]))
    out = run_ffn1(xn1, x1, attn1,
                   np.asarray(inputs["L1_ffn_qweight_gu"]),
                   np.asarray(inputs["L1_ffn_qzeros_gu"]),
                   np.asarray(inputs["L1_ffn_scales_gu"]),
                   np.asarray(inputs["L1_ffn_qweight_dn"]),
                   np.asarray(inputs["L1_ffn_qzeros_dn"]),
                   np.asarray(inputs["L1_ffn_scales_dn"]))
    return out
